# version 02
# baseline (speedup 1.0000x reference)
import numpy as np
from contextlib import ExitStack

import concourse.bass as bass
import concourse.mybir as mybir
from concourse.bass_utils import run_bass_kernel_spmd

F32 = mybir.dt.float32
BF16 = mybir.dt.bfloat16
I8 = mybir.dt.int8
AF = mybir.ActivationFunctionType
ALU = mybir.AluOpType

NCORES = 8
TOK = 2048
C = 2048
KT = C // 128
NB = TOK // 512
EPS = 1e-5

ENGINES = ("sync", "tensor", "vector", "scalar", "gpsimd")


class Sem:
    __slots__ = ("name", "n", "h")

    def __init__(self, name, handle):
        self.name, self.n, self.h = name, 0, handle


class _Inst:
    __slots__ = ("entry",)

    def __init__(self, entry):
        self.entry = entry

    def then_inc(self, sem, k=1):
        self.entry[3].append((sem, k))
        sem.n += k
        return self


class _Rec:
    def __init__(self, prog, name):
        self._p, self._n = prog, name

    def wait(self, sem, val=None):
        v = sem.n if val is None else val
        if v <= 0:
            return
        self._p.ops[self._n].append(("wait_ge", (sem, v), {}, None))

    def __getattr__(self, meth):
        p, n = self._p, self._n

        def rec(*a, **kw):
            entry = [meth, a, kw, []]
            p.ops[n].append(entry)
            return _Inst(entry)

        return rec


class Prog:
    def __init__(self):
        self.nc = bass.Bass("TRN2", target_bir_lowering=False, debug=False)
        self.ops = {e: [] for e in ENGINES}
        self._ctx = ExitStack()
        self.sync = _Rec(self, "sync")
        self.tensor = _Rec(self, "tensor")
        self.vector = _Rec(self, "vector")
        self.scalar = _Rec(self, "scalar")
        self.gpsimd = _Rec(self, "gpsimd")

    def dram(self, name, shape, dtype, kind="Internal", **kw):
        return self.nc.dram_tensor(name, shape, dtype, kind=kind, **kw)

    def sbuf(self, name, shape, dtype):
        return self._ctx.enter_context(self.nc.sbuf_tensor(name, shape, dtype))

    def psum(self, name, shape, dtype=F32):
        return self._ctx.enter_context(self.nc.psum_tensor(name, shape, dtype))

    def sem(self, name):
        return Sem(name, self._ctx.enter_context(self.nc.semaphore(name)))

    def finish(self):
        def make(ops):
            def body(eng):
                for item in ops:
                    meth, a, kw, incs = item
                    if meth == "wait_ge":
                        eng.wait_ge(a[0].h, a[1])
                        continue
                    inst = getattr(eng, meth)(*a, **kw)
                    if incs:
                        for s, k in incs:
                            inst.then_inc(s.h, k)
            return body

        with self.nc.Block() as block:
            for e in ENGINES:
                if self.ops[e]:
                    getattr(block, e)(make(self.ops[e]))
        self._ctx.close()
        return self.nc


class SemPool:

    def __init__(self, p, name, n):
        self.sems = [p.sem(f"{name}{i}") for i in range(n)]
        self.n = n

    def sem(self, j):
        return self.sems[j % self.n]

    def wait_all(self, rec):
        for s in self.sems:
            rec.wait(s)


def emit_consts(p, B):
    cn = p.sem("cn")
    v = p.vector
    v.memset(B["cpool"][:, 0:1], EPS)
    v.memset(B["cpool"][:, 1:2], -0.5)
    v.memset(B["cpool"][:, 2:3], 1.0 / 2048.0)
    v.memset(B["cpool"][:, 3:4], 0.5)
    v.memset(B["cpool"][:, 4:5], 1.0 / 4096.0)
    v.memset(B["cpool"][:, 5:6], -1.0e30)
    v.memset(B["cpool"][:, 6:7], -1.0)
    v.memset(B["cpool"][:, 7:8], 0.0)
    v.memset(B["cpool"][:, 8:9], 1.0e-6)
    v.memset(B["cpool"][:, 9:10], 10.0)
    v.memset(B["cpool"][:, 10:11], -10.0)
    v.memset(B["cpool"][:, 11:12], 1.0)
    v.memset(B["cpool"][:, 12:13], 100.0)
    v.memset(B["cpool"][:, 13:14], -100.0)
    v.memset(B["ones128"][:, :], 1.0).then_inc(cn, 1)
    return cn


def emit_layernorm(p, B, sems, rin, name):
    cn = sems["cn"]
    if "lnld" not in sems:
        sems["lnld"] = p.sem("lnld")
    ldp = sems["lnld"]
    sq = p.sem(name + "_sq")
    stm = p.sem(name + "_stm")
    stp = p.sem(name + "_stp")
    vtp = p.sem(name + "_vtp")
    nrm = p.sem(name + "_nrm")
    xb, xsq2, ps = B["xb"], B["xsq2"], B["ps"]
    mt, invt, vart, msq = B["mt"], B["invt"], B["vart"], B["msq"]

    for k in range(KT):
        p.gpsimd.dma_start(xb[:, k * TOK:(k + 1) * TOK],
                           rin[k * 128:(k + 1) * 128, :]).then_inc(ldp, 16)
    ld_all = ldp.n
    for k in range(KT):
        p.scalar.wait(ldp, ld_all)
        if k >= 1:
            p.scalar.wait(stm, k)
        p.scalar.square(xsq2[:, 0:TOK],
                        xb[:, k * TOK:(k + 1) * TOK]).then_inc(sq, 1)
    p.tensor.wait(cn)
    for k in range(KT):
        p.tensor.wait(sq, k + 1)
        for nb in range(NB):
            p.tensor.matmul(ps[nb][:, :], B["ones128"][:, :],
                            xb[:, k * TOK + nb * 512:k * TOK + (nb + 1) * 512],
                            start=(k == 0), stop=(k == KT - 1))
            last = p.tensor.matmul(ps[4 + nb][:, :], B["ones128"][:, :],
                            xsq2[:, nb * 512:(nb + 1) * 512],
                            start=(k == 0), stop=(k == KT - 1))
        last.then_inc(stm, 1)
    p.scalar.wait(stm, KT)
    p.scalar.wait(cn)
    sch = p.sem(name + "_sch")
    for nb in range(NB):
        sl = slice(nb * 512, (nb + 1) * 512)
        p.scalar.activation(mt[:, sl], ps[nb][:, :], AF.Copy,
                            scale=B["cpool"][:, 2:3]).then_inc(sch, 1)
        p.scalar.wait(sch)
        p.scalar.square(msq[:, :], mt[:, sl])
        p.scalar.activation(vart[:, :], ps[4 + nb][:, :], AF.Copy,
                            scale=B["cpool"][:, 2:3]).then_inc(stp, 1)
        p.vector.wait(stp, 2 * nb + 1)
        p.vector.tensor_sub(vart[:, :], vart[:, :], msq[:, :]).then_inc(vtp, 1)
        p.scalar.wait(vtp, nb + 1)
        p.scalar.activation(vart[:, :], vart[:, :], AF.Ln,
                            bias=B["cpool"][:, 0:1]).then_inc(sch, 1)
        p.scalar.wait(sch)
        p.scalar.activation(invt[:, sl], vart[:, :], AF.Exp,
                            scale=B["cpool"][:, 1:2]).then_inc(stp, 1)
    p.vector.wait(stp, 8)
    p.vector.wait(stm, KT)
    vch = p.sem(name + "_vch")
    for k in range(KT):
        for nb in range(NB):
            xs = slice(k * TOK + nb * 512, k * TOK + (nb + 1) * 512)
            ts = slice(nb * 512, (nb + 1) * 512)
            p.vector.tensor_sub(xb[:, xs], xb[:, xs], mt[:, ts]).then_inc(vch, 1)
            p.vector.wait(vch)
            lastv = p.vector.tensor_mul(xb[:, xs], xb[:, xs], invt[:, ts])
    lastv.then_inc(nrm, 1)
    return nrm


L1_INTER = 8192
L1_MT = L1_INTER // 128
L1_MG = 2
L1_NG = L1_MT // L1_MG
L1_KT = L1_INTER // 128


def emit_l1ffn(p, B, sems, io, name):
    nrm = emit_layernorm(p, B, sems, io["rin"], name + "ln")
    xb, ps = B["xb"], B["ps"]
    wgbuf, wubuf = B["wgbuf"], B["wubuf"]
    gtmp, htile = B["gtmp"], B["htile"]

    wldB = [p.sem(name + "_wldB0"), p.sem(name + "_wldB1")]
    pbd = p.sem(name + "_pbd")
    ev1 = p.sem(name + "_ev1")
    hv = p.sem(name + "_hv")
    hstp = SemPool(p, name + "_hst", 4)

    GW = L1_MG * 128
    NIDX = L1_NG * NB * L1_MG

    p.tensor.wait(nrm, 1)
    for g in range(L1_NG):
        hh = g % 2
        if g >= 2:
            p.gpsimd.wait(pbd, 8 * (g - 1))
        for j in range(4):
            for wio, wbuf in ((io["wg"], wgbuf), (io["wu"], wubuf)):
                srcw = wio[j * 512:(j + 1) * 512, g * GW:(g + 1) * GW]
                srcw = srcw.rearrange("(k p) c -> p k c", p=128)
                dst = wbuf[:, (hh * KT + j * 4) * GW:(hh * KT + (j + 1) * 4) * GW]
                p.gpsimd.dma_start(dst.rearrange("p (k c) -> p k c", k=4),
                                   srcw).then_inc(wldB[hh], 16)
        p.tensor.wait(wldB[hh])
        for nb in range(NB):
            for m in range(L1_MG):
                idx = (g * NB + nb) * L1_MG + m
                bp = idx % 4
                mtg = g * L1_MG + m
                if idx >= 4:
                    p.tensor.wait(ev1, idx - 3)
                    p.tensor.wait(hv, idx - 3)
                for k in range(KT):
                    p.tensor.matmul(
                        ps[bp][:, :],
                        wgbuf[:, (hh * KT + k) * GW + m * 128:(hh * KT + k) * GW + (m + 1) * 128],
                        xb[:, k * TOK + nb * 512:k * TOK + (nb + 1) * 512],
                        start=(k == 0), stop=(k == KT - 1))
                for k in range(KT):
                    inst = p.tensor.matmul(
                        ps[4 + bp][:, :],
                        wubuf[:, (hh * KT + k) * GW + m * 128:(hh * KT + k) * GW + (m + 1) * 128],
                        xb[:, k * TOK + nb * 512:k * TOK + (nb + 1) * 512],
                        start=(k == 0), stop=(k == KT - 1))
                inst.then_inc(pbd, 1)
                ssl = slice(bp * 512, (bp + 1) * 512)
                p.scalar.wait(pbd, idx + 1)
                if idx >= 4:
                    p.scalar.wait(hv, idx - 3)
                p.scalar.activation(gtmp[:, ssl], ps[bp][:, :], AF.Silu,
                                    scale=B["sgs"][:, mtg:mtg + 1]).then_inc(ev1, 1)
                p.vector.wait(ev1, idx + 1)
                if idx >= 4:
                    p.vector.wait(hstp.sem(idx))
                p.vector.scalar_tensor_tensor(htile[:, ssl], ps[4 + bp][:, :],
                                              B["sus"][:, mtg:mtg + 1], gtmp[:, ssl],
                                              ALU.mult, ALU.mult).then_inc(hv, 1)
                p.sync.wait(hv, idx + 1)
                p.sync.dma_start(
                    io["h"][mtg * 128:(mtg + 1) * 128, nb * 512:(nb + 1) * 512],
                    htile[:, ssl]).then_inc(hstp.sem(idx), 16)

    hnb, wdbuf = B["hnb"], B["wdbuf"]
    dtmp, rbuf = B["dtmp"], B["rbuf"]
    hld = p.sem(name + "_hld")
    wldC = SemPool(p, name + "_wldC", 4)
    wcons = p.sem(name + "_wcons")
    evC = p.sem(name + "_evC")
    radd = p.sem(name + "_radd")
    rldp = SemPool(p, name + "_rld", 4)
    rstp = SemPool(p, name + "_rst", 4)
    wdval = {}

    hstp.wait_all(p.gpsimd)
    p.tensor.wait(ev1, NIDX)
    p.tensor.wait(hv, NIDX)
    for nb in range(NB):
        if nb >= 1:
            p.gpsimd.wait(wcons, 16 * (nb * 2))
        for j in range(8):
            srch = io["h"][j * 1024:(j + 1) * 1024, nb * 512:(nb + 1) * 512]
            srch = srch.rearrange("(k p) c -> p k c", p=128)
            p.gpsimd.dma_start(hnb[:, (j * 8) * 512:(j * 8 + 8) * 512].rearrange(
                "p (k c) -> p k c", k=8), srch).then_inc(hld, 16)
        p.tensor.wait(hld, 128 * (nb + 1))
        for oset in range(2):
            sweep = nb * 2 + oset
            for j in range(16):
                gj = sweep * 16 + j
                if gj >= 4:
                    p.gpsimd.wait(wcons, gj - 3)
                srcw = io["wd"][j * 512:(j + 1) * 512, oset * 1024:(oset + 1) * 1024]
                srcw = srcw.rearrange("(k p) c -> p k c", p=128)
                dst = wdbuf[:, (gj % 4) * 4096:(gj % 4 + 1) * 4096]
                s = wldC.sem(gj)
                p.gpsimd.dma_start(dst.rearrange("p (k c) -> p k c", k=4),
                                   srcw).then_inc(s, 16)
                wdval[gj] = s.n
            if sweep >= 1:
                p.tensor.wait(evC, 8 * sweep)
            for k in range(L1_KT):
                gj = sweep * 16 + k // 4
                if k % 4 == 0:
                    p.tensor.wait(wldC.sem(gj), wdval[gj])
                wof = (gj % 4) * 4096 + (k % 4) * 1024
                for o in range(8):
                    inst = p.tensor.matmul(
                        ps[o][:, :],
                        wdbuf[:, wof + o * 128:wof + (o + 1) * 128],
                        hnb[:, k * 512:(k + 1) * 512],
                        start=(k == 0), stop=(k == L1_KT - 1))
                if k % 4 == 3:
                    inst.then_inc(wcons, 1)
    for i in range(4):
        p.sync.dma_start(rbuf[:, i * 512:(i + 1) * 512],
                         io["rin"][i * 128:(i + 1) * 128, 0:512]).then_inc(rldp.sem(i), 16)
    for i64 in range(64):
        sweep, o = divmod(i64, 8)
        nb, oset = divmod(sweep, 2)
        so = i64 % 4
        orow = (oset * 8 + o) * 128
        p.scalar.wait(wcons, 16 * (sweep + 1))
        if i64 >= 4:
            p.scalar.wait(radd, i64 - 3)
        p.scalar.activation(dtmp[:, so * 512:(so + 1) * 512], ps[o][:, :], AF.Copy,
                            scale=B["sd05"][:, oset * 8 + o:oset * 8 + o + 1]).then_inc(evC, 1)
        p.vector.wait(evC, i64 + 1)
        p.vector.wait(rldp.sem(i64))
        p.vector.tensor_add(rbuf[:, so * 512:(so + 1) * 512],
                            dtmp[:, so * 512:(so + 1) * 512],
                            rbuf[:, so * 512:(so + 1) * 512]).then_inc(radd, 1)
        p.sync.wait(radd, i64 + 1)
        p.sync.dma_start(io["rout"][orow:orow + 128, nb * 512:(nb + 1) * 512],
                         rbuf[:, so * 512:(so + 1) * 512]).then_inc(rstp.sem(i64), 16)
        nxt = i64 + 4
        if nxt < 64:
            nsweep, no = divmod(nxt, 8)
            nnb, noset = divmod(nsweep, 2)
            p.sync.wait(rstp.sem(i64))
            norow = (noset * 8 + no) * 128
            p.sync.dma_start(rbuf[:, (nxt % 4) * 512:(nxt % 4 + 1) * 512],
                             io["rin"][norow:norow + 128, nnb * 512:(nnb + 1) * 512]
                             ).then_inc(rldp.sem(nxt), 16)
    rstp.wait_all(p.gpsimd)
    return None


def alloc_common_bufs(p):
    B = {}
    B["cpool"] = p.sbuf("cpool", [128, 16], F32)
    B["ones128"] = p.sbuf("ones128", [128, 128], BF16)
    B["xsq2"] = p.sbuf("xsq2", [128, TOK], BF16)
    msc = p.sbuf("mscr", [128, 2 * TOK], BF16)
    B["mscratch"] = msc
    B["mt"] = msc[:, 0:TOK]
    B["invt"] = msc[:, TOK:2 * TOK]
    B["vart"] = p.sbuf("vart", [128, 512], F32)
    B["msq"] = p.sbuf("msq", [128, 512], F32)
    B["ps"] = [p.psum(f"ps{i}", [128, 512], F32) for i in range(8)]
    return B


def alloc_l1ffn_bufs(p, B):
    B["xb"] = p.sbuf("xb", [128, KT * TOK], BF16)
    big = p.sbuf("big", [128, 64 * 512 + 4 * 4096], BF16)
    B["wgbuf"] = big[:, 0:2 * KT * 256]
    B["wubuf"] = big[:, 2 * KT * 256:4 * KT * 256]
    B["hnb"] = big[:, 0:64 * 512]
    B["wdbuf"] = big[:, 64 * 512:64 * 512 + 4 * 4096]
    B["sgs"] = p.sbuf("sgs", [128, 64], F32)
    B["sus"] = p.sbuf("sus", [128, 64], F32)
    B["sds"] = p.sbuf("sds", [128, 16], F32)
    B["sd05"] = p.sbuf("sd05", [128, 16], F32)
    B["gtmp"] = p.sbuf("gtmp", [128, 4 * 512], BF16)
    B["htile"] = p.sbuf("htile", [128, 4 * 512], BF16)
    B["dtmp"] = p.sbuf("dtmp", [128, 4 * 512], BF16)
    B["rbuf"] = p.sbuf("rbuf", [128, 4 * 512], F32)
    return B


def build_l1ffn():
    p = Prog()
    io = {}
    io["rin"] = p.dram("rin", [C, TOK], F32, kind="ExternalInput")
    io["wg"] = p.dram("wg", [C, L1_INTER], I8, kind="ExternalInput")
    io["wu"] = p.dram("wu", [C, L1_INTER], I8, kind="ExternalInput")
    io["wd"] = p.dram("wd", [L1_INTER, C], I8, kind="ExternalInput")
    io["sg2d"] = p.dram("sg2d", [128, 64], F32, kind="ExternalInput")
    io["su2d"] = p.dram("su2d", [128, 64], F32, kind="ExternalInput")
    io["sd2d"] = p.dram("sd2d", [128, 16], F32, kind="ExternalInput")
    io["rout"] = p.dram("rout", [C, TOK], F32, kind="ExternalOutput")
    io["h"] = p.dram("hscratch", [L1_INTER, TOK], BF16, kind="Internal")

    B = alloc_common_bufs(p)
    alloc_l1ffn_bufs(p, B)
    sems = {"cn": emit_consts(p, B)}
    ld2 = p.sem("ld2")
    p.sync.dma_start(B["sgs"][:, :], io["sg2d"][:, :]).then_inc(ld2, 16)
    p.sync.dma_start(B["sus"][:, :], io["su2d"][:, :]).then_inc(ld2, 16)
    p.sync.dma_start(B["sds"][:, :], io["sd2d"][:, :]).then_inc(ld2, 16)
    sc05 = p.sem("sc05")
    p.vector.wait(ld2, 48)
    p.vector.wait(sems["cn"], None)
    p.vector.tensor_scalar(B["sd05"][:, :], B["sds"][:, :], B["cpool"][:, 3:4], None,
                           ALU.mult).then_inc(sc05, 1)
    p.scalar.wait(ld2, 48)
    p.scalar.wait(sc05, 1)
    emit_l1ffn(p, B, sems, io, "f")
    return p.finish()


L0_HROWS = 14080
L0_KT2 = L0_HROWS // 128
L0_HALF = 55


def _l0_pairs():
    pairs = []
    for i in range(22):
        pairs.append((i, 22 + i, i, 0, -1))
    for e in range(8):
        base = 44 + e * 22
        for j in range(11):
            pairs.append((base + j, base + 11 + j, 22 + e * 11 + j, 1, e))
    return pairs


def emit_l0ffn(p, B, sems, io, name, upto="full"):
    nrm = emit_layernorm(p, B, sems, io["rin"], name + "ln")
    xb, ps = B["xb"], B["ps"]
    cn = sems["cn"]
    X = mybir.AxisListType.X

    gld = p.sem(name + "_gld")
    gm1 = p.sem(name + "_gm1")
    gsg = p.sem(name + "_gsg")
    gcp = p.sem(name + "_gcp")
    gmk = p.sem(name + "_gmk")
    gtb = p.sem(name + "_gtb")
    gwev = p.sem(name + "_gwev")
    gwm = p.sem(name + "_gwm")
    gwbc = p.sem(name + "_gwbc")
    gvv = p.sem(name + "_gvv")
    sch2 = p.sem(name + "_sch2")
    gwb, scsb, scbb = B["gwb"], B["scsb"], B["scbb"]
    gbc, idb, tm = B["gbc"], B["idf"], B["tm"]
    wec, web = B["wec"], B["web"]
    NQ = TOK // 128

    gldg = p.sem(name + "_gldg")
    p.gpsimd.dma_start(gwb[:, :], io["gw2d"][:, :]).then_inc(gldg, 16)
    p.sync.dma_start(gbc[:, :], io["gbias2d"][:, :]).then_inc(gld, 16)
    p.sync.dma_start(idb[:, :], io["id128"][:, :]).then_inc(gld, 16)
    p.sync.dma_start(B["sel8"][:, :], io["sel8"][:, :]).then_inc(gld, 16)
    p.tensor.wait(nrm, 1)
    p.tensor.wait(gld, 48)
    p.tensor.wait(gldg, 16)
    for nb in range(NB):
        for k in range(KT):
            inst = p.tensor.matmul(ps[nb][0:8, :], gwb[:, k * 8:(k + 1) * 8],
                                   xb[:, k * TOK + nb * 512:k * TOK + (nb + 1) * 512],
                                   start=(k == 0), stop=(k == KT - 1))
        inst.then_inc(gm1, 1)
        p.scalar.wait(gm1, nb + 1)
        p.scalar.activation(scsb[0:8, nb * 512:(nb + 1) * 512], ps[nb][0:8, :],
                            AF.Sigmoid).then_inc(sch2, 1)
        p.vector.wait(sch2, nb + 1)
        p.vector.tensor_scalar(scbb[0:8, nb * 512:(nb + 1) * 512],
                               scsb[0:8, nb * 512:(nb + 1) * 512],
                               gbc[0:8, 0:1], None, ALU.add).then_inc(gsg, 1)
    for q in range(NQ):
        cs = slice(q * 128, (q + 1) * 128)
        p.tensor.wait(gsg, q // 4 + 1)
        if q >= 1:
            p.tensor.wait(gcp, q)
        p.tensor.transpose(ps[4][:, 0:8], scsb[0:8, cs], idb[0:8, 0:8])
        p.tensor.transpose(ps[5][:, 0:8], scbb[0:8, cs], idb[0:8, 0:8]).then_inc(gm1, 1)
        st_, tt_ = tm[:, 0:8], tm[:, 8:16]
        m1, m2 = tm[:, 16:17], tm[:, 17:18]
        msk = tm[:, 24:32]
        p.vector.wait(gm1, NB + q + 1)
        p.vector.tensor_copy(st_, ps[4][:, 0:8])
        p.vector.tensor_copy(tt_, ps[5][:, 0:8]).then_inc(gcp, 1)
        p.vector.wait(gcp)
        p.vector.reduce_max(m1, tt_, axis=X).then_inc(gvv, 1)
        p.vector.wait(gvv)
        p.vector.tensor_scalar(msk, tt_, m1[:, 0:1], None, ALU.is_ge).then_inc(gvv, 1)
        p.vector.wait(gvv)
        p.vector.scalar_tensor_tensor(msk, msk, B["cpool"][:, 5:6], tt_,
                                      ALU.mult, ALU.add).then_inc(gvv, 1)
        p.vector.wait(gvv)
        p.vector.reduce_max(m2, msk, axis=X).then_inc(gvv, 1)
        p.vector.wait(gvv)
        p.vector.tensor_scalar(msk, tt_, m2[:, 0:1], None, ALU.is_ge).then_inc(gvv, 1)
        p.vector.wait(gvv)
        p.vector.tensor_mul(msk, msk, st_).then_inc(gvv, 1)
        p.vector.wait(gvv)
        p.vector.reduce_sum(m1, msk, axis=X).then_inc(gvv, 1)
        p.vector.wait(gvv)
        p.vector.reciprocal(m2, m1[:, 0:1]).then_inc(gvv, 1)
        p.vector.wait(gvv)
        p.vector.tensor_scalar(msk, msk, m2[:, 0:1], None, ALU.mult).then_inc(gmk, 1)
        p.tensor.wait(gmk, q + 1)
        if q >= 1:
            p.tensor.wait(gwev, q)
        p.tensor.transpose(ps[6][0:8, 0:128], msk, idb[:, :]).then_inc(gtb, 1)
        p.scalar.wait(gtb, q + 1)
        p.scalar.activation(wec[0:8, cs], ps[6][0:8, 0:128], AF.Copy).then_inc(gwev, 1)
    p.tensor.wait(gwev, NQ)
    for e in range(8):
        for nb in range(NB):
            i = e * NB + nb
            if i >= 1:
                p.tensor.wait(gwbc, i)
            p.tensor.matmul(ps[7][:, :], B["sel8"][:, e * 128:(e + 1) * 128],
                            wec[0:8, nb * 512:(nb + 1) * 512],
                            start=True, stop=True).then_inc(gwm, 1)
            p.scalar.wait(gwm, i + 1)
            p.scalar.activation(web[:, e * TOK + nb * 512:e * TOK + (nb + 1) * 512],
                                ps[7][:, :], AF.Copy).then_inc(gwbc, 1)
    gdone = gwbc
    if upto == "gate":
        fin = p.sem(name + "_fin")
        p.sync.wait(gdone, 32)
        for k in range(KT):
            p.sync.dma_start(io["rout"][k * 128:(k + 1) * 128, :],
                             io["rin"][k * 128:(k + 1) * 128, :]).then_inc(fin, 16)
        p.gpsimd.wait(fin, 16 * KT)
        return None

    w1b = B["w1b"]
    gtmp, t1t, htile = B["gtmp"], B["t1t"], B["htile"]
    wldU = [p.sem(name + "_wldU0"), p.sem(name + "_wldU1")]
    upd = p.sem(name + "_upd")
    ev1 = p.sem(name + "_ev1")
    hv = p.sem(name + "_hv")
    vch = p.sem(name + "_vch")
    hstp = SemPool(p, name + "_hst", 4)
    pairs = _l0_pairs()
    NUP = len(pairs) * NB
    p.tensor.wait(gdone, 32)
    p.vector.wait(gdone, 32)
    for pi, (mA, mB, hrow, kind, e) in enumerate(pairs):
        par = pi % 2
        if pi >= 2:
            p.gpsimd.wait(upd, 4 * (pi - 1))
        for side, mcol in ((0, mA), (1, mB)):
            for j in range(4):
                srcw = io["w1"][j * 512:(j + 1) * 512, mcol * 128:(mcol + 1) * 128]
                srcw = srcw.rearrange("(k p) c -> p k c", p=128)
                base = ((par * 2 + side) * 16 + j * 4) * 128
                dst = w1b[:, base:base + 4 * 128]
                p.gpsimd.dma_start(dst.rearrange("p (k c) -> p k c", k=4),
                                   srcw).then_inc(wldU[par], 16)
        p.tensor.wait(wldU[par])
        for nb in range(NB):
            idx = pi * NB + nb
            bp = idx % 4
            if idx >= 4:
                p.tensor.wait(ev1, idx - 3)
                p.tensor.wait(hv, idx - 3)
            for k in range(KT):
                p.tensor.matmul(
                    ps[bp][:, :],
                    w1b[:, ((par * 2) * 16 + k) * 128:((par * 2) * 16 + k + 1) * 128],
                    xb[:, k * TOK + nb * 512:k * TOK + (nb + 1) * 512],
                    start=(k == 0), stop=(k == KT - 1))
            for k in range(KT):
                inst = p.tensor.matmul(
                    ps[4 + bp][:, :],
                    w1b[:, ((par * 2 + 1) * 16 + k) * 128:((par * 2 + 1) * 16 + k + 1) * 128],
                    xb[:, k * TOK + nb * 512:k * TOK + (nb + 1) * 512],
                    start=(k == 0), stop=(k == KT - 1))
            inst.then_inc(upd, 1)
            ssl = slice(bp * 512, (bp + 1) * 512)
            p.scalar.wait(upd, idx + 1)
            if idx >= 4:
                p.scalar.wait(hv, idx - 3)
            gps = ps[4 + bp] if kind == 0 else ps[bp]
            ops = ps[bp] if kind == 0 else ps[4 + bp]
            p.scalar.activation(gtmp[:, ssl], gps[:, :], AF.Silu).then_inc(ev1, 1)
            p.vector.wait(ev1, idx + 1)
            if idx >= 4:
                p.vector.wait(hstp.sem(idx))
            if kind == 0:
                p.vector.tensor_mul(htile[:, ssl], gtmp[:, ssl], ops[:, :]).then_inc(hv, 1)
            else:
                p.vector.tensor_mul(t1t[:, ssl], gtmp[:, ssl], ops[:, :]).then_inc(vch, 1)
                p.vector.wait(vch)
                p.vector.tensor_mul(
                    htile[:, ssl], t1t[:, ssl],
                    web[:, e * TOK + nb * 512:e * TOK + (nb + 1) * 512]).then_inc(hv, 1)
            p.sync.wait(hv, idx + 1)
            p.sync.dma_start(
                io["h"][hrow * 128:(hrow + 1) * 128, nb * 512:(nb + 1) * 512],
                htile[:, ssl]).then_inc(hstp.sem(idx), 16)

    hhf, w2b = B["hhf"], B["w2b"]
    dtmp, rbuf, aptile = B["dtmp"], B["rbuf"], B["aptile"]
    hld = p.sem(name + "_hld")
    wldD = SemPool(p, name + "_wldD", 4)
    wcons = p.sem(name + "_wcons")
    evC = p.sem(name + "_evC")
    radd = p.sem(name + "_radd")
    rldp = SemPool(p, name + "_rld", 4)
    rstp = SemPool(p, name + "_rst", 8)
    apld = SemPool(p, name + "_apl", 4)
    wdval = {}
    QTS = (28, 28, 27, 27)

    if upto == "up":
        fin = p.sem(name + "_fin")
        hstp.wait_all(p.sync)
        for k in range(KT):
            p.sync.dma_start(io["rout"][k * 128:(k + 1) * 128, :],
                             io["rin"][k * 128:(k + 1) * 128, :]).then_inc(fin, 16)
        p.gpsimd.wait(fin, 16 * KT)
        hstp.wait_all(p.gpsimd)
        return None
    hstp.wait_all(p.gpsimd)
    p.tensor.wait(ev1, NUP)
    p.tensor.wait(hv, NUP)
    kbase = 0
    swp = 0
    i64 = 0
    wc = 0
    hldv = 0
    NQT = int(upto[4:]) if upto.startswith("down") else 4
    for qt in range(NQT):
        nk = QTS[qt]
        nbat = (nk + 3) // 4
        if qt >= 1:
            p.gpsimd.wait(wcons, wc)
            for j in range(4):
                otile, nb = divmod(j, 4)
                orow = otile * 128
                csl = slice(nb * 512, (nb + 1) * 512)
                p.sync.wait(rstp.sem(i64 - 4 + j))
                p.sync.dma_start(aptile[:, j * 512:(j + 1) * 512],
                                 io["apart"][orow:orow + 128, csl]).then_inc(
                                     apld.sem((qt - 1) * 64 + j), 16)
                if qt == 3:
                    p.sync.dma_start(rbuf[:, j * 512:(j + 1) * 512],
                                     io["rin"][orow:orow + 128, csl]).then_inc(
                                         rldp.sem(j), 16)
        for j in range(nbat):
            k0, k1 = j * 4, min((j + 1) * 4, nk)
            srch = io["h"][(kbase + k0) * 128:(kbase + k1) * 128, :]
            srch = srch.rearrange("(k p) c -> p k c", p=128)
            p.gpsimd.dma_start(hhf[:, k0 * TOK:k1 * TOK].rearrange(
                "p (k c) -> p k c", k=k1 - k0), srch).then_inc(hld, 16)
            hldv += 16
        p.tensor.wait(hld, hldv)
        if upto == "q0h":
            fin = p.sem(name + "_fin")
            p.sync.wait(hld, hldv)
            for k in range(KT):
                p.sync.dma_start(io["rout"][k * 128:(k + 1) * 128, :],
                                 io["rin"][k * 128:(k + 1) * 128, :]).then_inc(fin, 16)
            p.gpsimd.wait(fin, 16 * KT)
            return None
        NOP_ = int(upto[3:]) if upto.startswith("q0s") else 8
        for op_ in range(NOP_):
            for j in range(nbat):
                gj = swp * 7 + j
                if gj >= 4:
                    p.gpsimd.wait(wcons, gj - 3)
                k0, k1 = j * 4, min((j + 1) * 4, nk)
                srcw = io["w2"][(kbase + k0) * 128:(kbase + k1) * 128,
                                op_ * 256:(op_ + 1) * 256]
                srcw = srcw.rearrange("(k p) c -> p k c", p=128)
                dst = w2b[:, (gj % 4) * 1024:(gj % 4) * 1024 + (k1 - k0) * 256]
                s = wldD.sem(gj)
                p.gpsimd.dma_start(dst.rearrange("p (k c) -> p k c", k=k1 - k0),
                                   srcw).then_inc(s, 16)
                wdval[gj] = s.n
            if swp >= 1:
                p.tensor.wait(evC, 8 * swp)
            for k in range(nk):
                gj = swp * 7 + k // 4
                if k % 4 == 0:
                    p.tensor.wait(wldD.sem(gj), wdval[gj])
                wof = (gj % 4) * 1024 + (k % 4) * 256
                for sub in range(8):
                    o, nb = divmod(sub, 4)
                    inst = p.tensor.matmul(
                        ps[sub][:, :],
                        w2b[:, wof + o * 128:wof + (o + 1) * 128],
                        hhf[:, k * TOK + nb * 512:k * TOK + (nb + 1) * 512],
                        start=(k == 0), stop=(k == nk - 1))
                if k % 4 == 3 or k == nk - 1:
                    inst.then_inc(wcons, 1)
            swpdone = wcons.n
            for sub in range(8):
                o, nb = divmod(sub, 4)
                otile = op_ * 2 + o
                so = i64 % 4
                orow = otile * 128
                csl = slice(nb * 512, (nb + 1) * 512)
                dsl = slice(so * 512, (so + 1) * 512)
                p.scalar.wait(wcons, swpdone)
                p.scalar.wait(radd, i64 - 3)
                p.scalar.activation(dtmp[:, dsl], ps[sub][:, :], AF.Copy).then_inc(evC, 1)
                p.vector.wait(evC, i64 + 1)
                if qt == 0:
                    if i64 >= 4:
                        p.vector.wait(rstp.sem(i64 - 4))
                    p.vector.tensor_copy(aptile[:, dsl], dtmp[:, dsl]).then_inc(radd, 1)
                    p.sync.wait(radd, i64 + 1)
                    p.sync.dma_start(io["apart"][orow:orow + 128, csl],
                                     aptile[:, dsl]).then_inc(rstp.sem(i64), 16)
                else:
                    jj = i64 % 64
                    p.vector.wait(apld.sem((qt - 1) * 64 + jj))
                    if qt < 3:
                        p.vector.tensor_add(aptile[:, dsl], dtmp[:, dsl],
                                            aptile[:, dsl]).then_inc(radd, 1)
                        p.sync.wait(radd, i64 + 1)
                        p.sync.dma_start(io["apart"][orow:orow + 128, csl],
                                         aptile[:, dsl]).then_inc(rstp.sem(i64), 16)
                    else:
                        p.vector.wait(rldp.sem(jj))
                        p.vector.tensor_add(dtmp[:, dsl], dtmp[:, dsl],
                                            aptile[:, dsl]).then_inc(vch, 1)
                        p.vector.wait(vch)
                        p.vector.scalar_tensor_tensor(rbuf[:, dsl], dtmp[:, dsl],
                                                      B["cpool"][:, 3:4], rbuf[:, dsl],
                                                      ALU.mult, ALU.add).then_inc(radd, 1)
                        p.sync.wait(radd, i64 + 1)
                        p.sync.dma_start(io["rout"][orow:orow + 128, csl],
                                         rbuf[:, dsl]).then_inc(rstp.sem(i64), 16)
                    nj = jj + 4
                    if nj < 64:
                        nop, nsub = divmod(nj, 8)
                        no, nnb = divmod(nsub, 4)
                        norow = (nop * 2 + no) * 128
                        ncsl = slice(nnb * 512, (nnb + 1) * 512)
                        p.sync.wait(rstp.sem(i64))
                        p.sync.dma_start(aptile[:, (nj % 4) * 512:(nj % 4 + 1) * 512],
                                         io["apart"][norow:norow + 128, ncsl]
                                         ).then_inc(apld.sem((qt - 1) * 64 + nj), 16)
                        if qt == 3:
                            p.sync.dma_start(rbuf[:, (nj % 4) * 512:(nj % 4 + 1) * 512],
                                             io["rin"][norow:norow + 128, ncsl]
                                             ).then_inc(rldp.sem(nj), 16)
                i64 += 1
            swp += 1
        if upto.startswith("q0s"):
            fin = p.sem(name + "_fin")
            rstp.wait_all(p.sync)
            p.sync.wait(radd, i64)
            for k in range(KT):
                p.sync.dma_start(io["rout"][k * 128:(k + 1) * 128, :],
                                 io["rin"][k * 128:(k + 1) * 128, :]).then_inc(fin, 16)
            p.gpsimd.wait(fin, 16 * KT)
            return None
        wc = wcons.n
        kbase += nk
    if NQT < 4:
        fin = p.sem(name + "_fin")
        rstp.wait_all(p.sync)
        for k in range(KT):
            p.sync.dma_start(io["rout"][k * 128:(k + 1) * 128, :],
                             io["rin"][k * 128:(k + 1) * 128, :]).then_inc(fin, 16)
        p.gpsimd.wait(fin, 16 * KT)
    rstp.wait_all(p.gpsimd)
    return None


def alloc_l0ffn_bufs(p, B):
    big = p.sbuf("big0", [128, 61440], BF16)
    B["xb"] = big[:, 0:32768]
    B["web"] = big[:, 32768:49152]
    B["w1b"] = big[:, 49152:57344]
    B["hhf"] = big[:, 0:57344]
    B["w2b"] = big[:, 57344:61440]
    B["gwb"] = p.sbuf("gwb", [128, 128], BF16)
    B["scsb"] = p.sbuf("scsb", [8, TOK], F32)
    B["scbb"] = p.sbuf("scbb", [8, TOK], F32)
    B["gbc"] = p.sbuf("gbc", [8, 1], F32)
    B["idf"] = p.sbuf("idf", [128, 128], F32)
    B["tm"] = p.sbuf("tm", [128, 32], F32)
    B["wec"] = p.sbuf("wec", [8, TOK], F32)
    B["sel8"] = p.sbuf("sel8", [8, 1024], F32)
    B["gtmp"] = p.sbuf("gtmp", [128, 4 * 512], BF16)
    B["t1t"] = p.sbuf("t1t", [128, 4 * 512], BF16)
    B["htile"] = p.sbuf("htile", [128, 4 * 512], BF16)
    B["dtmp"] = p.sbuf("dtmp", [128, 4 * 512], F32)
    B["aptile"] = p.sbuf("aptile", [128, 4 * 512], F32)
    B["rbuf"] = p.sbuf("rbuf", [128, 4 * 512], F32)
    return B


def build_l0ffn(upto="full"):
    p = Prog()
    io = {}
    io["rin"] = p.dram("rin", [C, TOK], F32, kind="ExternalInput")
    io["w1"] = p.dram("w1", [C, 28160], F32, kind="ExternalInput")
    io["w2"] = p.dram("w2", [L0_HROWS, C], F32, kind="ExternalInput")
    io["gw2d"] = p.dram("gw2d", [128, 128], F32, kind="ExternalInput")
    io["gbias2d"] = p.dram("gbias2d", [8, 1], F32, kind="ExternalInput")
    io["id128"] = p.dram("id128", [128, 128], F32, kind="ExternalInput")
    io["sel8"] = p.dram("sel8i", [8, 1024], F32, kind="ExternalInput")
    io["rout"] = p.dram("rout", [C, TOK], F32, kind="ExternalOutput")
    io["h"] = p.dram("hscratch", [L0_HROWS, TOK], BF16, kind="Internal")
    io["apart"] = p.dram("apart", [C, TOK], F32, kind="Internal")
    B = alloc_common_bufs(p)
    alloc_l0ffn_bufs(p, B)
    sems = {"cn": emit_consts(p, B)}
    emit_l0ffn(p, B, sems, io, "m", upto=upto)
    return p.finish()


def shard_tokens(x):
    xf = np.ascontiguousarray(x.reshape(NCORES * TOK, C))
    return [np.ascontiguousarray(xf[k * TOK:(k + 1) * TOK].T) for k in range(NCORES)]


def unshard_tokens(parts):
    xf = np.concatenate([p.T for p in parts], axis=0)
    return np.ascontiguousarray(xf.reshape(4, 4096, C))


def scale_cols(s):
    n = s.shape[0] // 128
    return np.ascontiguousarray(s.reshape(n, 128).T)


MCH = 128
MNC = TOK // MCH
MNH = 64
MP = 64
MN = 64


def emit_rms(p, B, sems, name, wcol_all, dep):
    cn = sems["cn"]
    xb, xsq2, ps = B["xb"], B["xsq2"], B["ps"]
    vart, invt = B["vart"], B["invt"]
    sq = p.sem(name + "_sq")
    stm = p.sem(name + "_stm")
    stp = p.sem(name + "_stp")
    nrm = p.sem(name + "_nrm")
    p.scalar.wait(dep, 1)
    p.tensor.wait(dep, 1)
    for k in range(KT):
        if k >= 1:
            p.scalar.wait(stm, k)
        p.scalar.square(xsq2[:, 0:TOK], xb[:, k * TOK:(k + 1) * TOK]).then_inc(sq, 1)
        p.tensor.wait(sq, k + 1)
        for nb in range(NB):
            last = p.tensor.matmul(ps[nb][:, :], B["ones128"][:, :],
                                   xsq2[:, nb * 512:(nb + 1) * 512],
                                   start=(k == 0), stop=(k == KT - 1))
        last.then_inc(stm, 1)
    for nb in range(NB):
        sl = slice(nb * 512, (nb + 1) * 512)
        p.scalar.wait(stm, KT)
        p.scalar.wait(cn)
        p.scalar.wait(stp)
        p.scalar.activation(vart[:, :], ps[nb][:, :], AF.Copy,
                            scale=B["cpool"][:, 2:3]).then_inc(stp, 1)
        p.scalar.wait(stp)
        p.scalar.activation(vart[:, :], vart[:, :], AF.Ln,
                            bias=B["cpool"][:, 0:1]).then_inc(stp, 1)
        p.scalar.wait(stp)
        p.scalar.activation(invt[:, sl], vart[:, :], AF.Exp,
                            scale=B["cpool"][:, 1:2]).then_inc(stp, 1)
    p.vector.wait(stp, 12)
    p.vector.wait(stm, KT)
    vch = p.sem(name + "_vch")
    for k in range(KT):
        for nb in range(NB):
            xs = slice(k * TOK + nb * 512, k * TOK + (nb + 1) * 512)
            p.vector.tensor_mul(xb[:, xs], xb[:, xs],
                                invt[:, nb * 512:(nb + 1) * 512]).then_inc(vch, 1)
            p.vector.wait(vch)
            lastv = p.vector.tensor_scalar(xb[:, xs], xb[:, xs],
                                           wcol_all[:, k:k + 1], None, ALU.mult)
    lastv.then_inc(nrm, 1)
    return nrm


def emit_mamba(p, B, sems, io, name):
    cn = sems["cn"]
    cp = B["cpool"]
    ps = B["ps"]
    X = mybir.AxisListType.X
    nrm0 = emit_layernorm(p, B, sems, io["rin"], name + "ln")

    cld = p.sem(name + "_cld")
    cldg = p.sem(name + "_cldg")
    sch = p.sem(name + "_sch")
    vch = p.sem(name + "_vch")
    mmc = p.sem(name + "_mmc")
    for dst, srckey in ((B["cwb"], "cw2d"), (B["cbb"], "cb2d"), (B["lnw"], "lnw2d"),
                        (B["nw"], "nw2d"),
                        (B["dtb"][0:64, :], "dtb2d"), (B["dtb"][64:128, :], "dtb2d"),
                        (B["alog"][0:64, :], "alog2d"), (B["alog"][64:128, :], "alog2d"),
                        (B["d1"], "d1r"),
                        (B["idpair"][0:64, :], "id64"), (B["idpair"][64:128, :], "id64"),
                        (B["musel"], "musel")):
        p.sync.dma_start(dst[:, :], io[srckey][:, :]).then_inc(cld, 16)
    p.gpsimd.dma_start(B["rep64"][0:64, 0:4096], io["rep64in"][:, :]).then_inc(cldg, 16)
    p.gpsimd.dma_start(B["idb"][:, :], io["id128"][:, :]).then_inc(cldg, 16)
    NCLD = 192
    p.vector.wait(cn)
    msk = p.sem(name + "_msk")
    onesf = B["onesf"]
    p.vector.memset(onesf[:, :], 1.0).then_inc(msk, 1)
    p.vector.memset(B["masksl"][:, :], 1.0).then_inc(msk, 1)
    p.gpsimd.wait(msk, 2)
    p.gpsimd.affine_select(B["masksl"][:, :], B["masksl"][:, :], [[1, 128]],
                           ALU.is_ge, 0.0, base=0, channel_multiplier=-1).then_inc(msk, 1)
    p.tensor.wait(cld, NCLD)
    p.tensor.wait(cn)
    p.tensor.wait(msk, 3)
    p.tensor.matmul(ps[7][:, 0:64], onesf[0:1, :], B["d1"][0:1, :],
                    start=True, stop=True).then_inc(mmc, 1)
    p.scalar.wait(mmc, 1)
    p.scalar.activation(B["dx"][:, :], ps[7][:, 0:64], AF.Copy).then_inc(sch, 1)

    mbF = B["mb"].bitcast(F32)
    rxld = SemPool(p, name + "_rxld", 4)
    rxst = SemPool(p, name + "_rxst", 4)
    rxv = p.sem(name + "_rxv")
    for k in range(KT):
        sl = k % 4
        slot = mbF[:, sl * 2048:(sl + 1) * 2048]
        if k >= 4:
            p.gpsimd.wait(rxst.sem(k - 4))
        p.gpsimd.dma_start(slot[:, :], io["rin"][k * 128:(k + 1) * 128, :]
                           ).then_inc(rxld.sem(k), 16)
        p.vector.wait(rxld.sem(k))
        p.vector.tensor_add(slot[:, :], slot[:, :],
                            B["xb"][:, k * TOK:(k + 1) * TOK]).then_inc(rxv, 1)
        p.sync.wait(rxv, k + 1)
        p.sync.dma_start(io["rpx"][k * 128:(k + 1) * 128, :],
                         slot[:, :]).then_inc(rxst.sem(k), 16)

    p.vector.wait(cld, NCLD)
    nrm = emit_rms(p, B, sems, name + "rms", B["lnw"], nrm0)

    xb = B["xb"]
    ipb = B["ipb"]
    xbcr = B["xbcr"]
    gt2 = B["gt2"]
    xs2 = B["xs2"]
    f3 = B["f3"]
    ag1s = B["ag1s"]
    dtp = B["dtpair"]
    wldM = [p.sem(name + "_wldM0"), p.sem(name + "_wldM1")]
    mmP = p.sem(name + "_mmP")
    evP = p.sem(name + "_evP")
    cvv = p.sem(name + "_cvv")
    gst = SemPool(p, name + "_gst", 4)
    xst = SemPool(p, name + "_xst", 4)
    mm_done = {}
    ev_done = {}
    slot_free = {}
    p.tensor.wait(nrm, 1)
    rxst.wait_all(p.scalar)
    rxst.wait_all(p.vector)
    rxst.wait_all(p.gpsimd)
    for mi in range(66):
        par = mi % 2
        mw = 64 if mi == 65 else 128
        if mi >= 2:
            p.gpsimd.wait(mmP, mm_done[mi - 2])
        for j in range(4):
            srcw = io["ipT"][j * 512:(j + 1) * 512, mi * 128:mi * 128 + mw]
            srcw = srcw.rearrange("(k p) c -> p k c", p=128)
            base = (par * 16 + j * 4) * 128
            dst = ipb[:, base:base + 4 * mw]
            p.gpsimd.dma_start(dst.rearrange("p (k c) -> p k c", k=4),
                               srcw).then_inc(wldM[par], 16)
        p.tensor.wait(wldM[par])
        if mi >= 2:
            p.tensor.wait(evP, ev_done[mi - 2])
        for nb in range(NB):
            bk = par * 4 + nb
            for k in range(KT):
                inst = p.tensor.matmul(
                    ps[bk][0:mw, :],
                    ipb[:, (par * 16 + k) * 128:(par * 16 + k) * 128 + mw],
                    xb[:, k * TOK + nb * 512:k * TOK + (nb + 1) * 512],
                    start=(k == 0), stop=(k == KT - 1))
            inst.then_inc(mmP, 1)
        mm_done[mi] = mmP.n
        if mi < 32:
            g2 = mi % 2
            for nb in range(NB):
                p.scalar.wait(mmP, mm_done[mi] - 3 + nb)
                if mi >= 2:
                    p.scalar.wait(gst.sem(mi - 2))
                p.scalar.activation(gt2[:, g2 * TOK + nb * 512:g2 * TOK + (nb + 1) * 512],
                                    ps[par * 4 + nb][:, :], AF.Copy).then_inc(evP, 1)
            ev_done[mi] = evP.n
            p.sync.wait(evP, ev_done[mi])
            p.sync.dma_start(io["gatehbm"][mi * 128:(mi + 1) * 128, :],
                             gt2[:, g2 * TOK:(g2 + 1) * TOK]).then_inc(gst.sem(mi), 16)
        elif mi < 65:
            ti = mi - 32
            r = ti % 2
            xr = xbcr[:, r * 2056:r * 2056 + 2051]
            if ti >= 2:
                p.vector.wait(evP, ev_done[mi - 2])
                p.vector.wait(cvv, slot_free[ti - 2])
            p.vector.memset(xr[:, 0:3], 0.0).then_inc(cvv, 1)
            mset = cvv.n
            for nb in range(NB):
                p.scalar.wait(mmP, mm_done[mi] - 3 + nb)
                if ti >= 2:
                    p.scalar.wait(cvv, slot_free[ti - 2])
                p.scalar.activation(xr[:, 3 + nb * 512:3 + (nb + 1) * 512],
                                    ps[par * 4 + nb][:, :], AF.Copy).then_inc(evP, 1)
            ev_done[mi] = evP.n
            p.vector.wait(evP, ev_done[mi])
            p.vector.wait(cld, NCLD)
            p.vector.wait(cvv, mset)
            acc = B["cacc"]
            p.vector.tensor_scalar(acc[:, :], xr[:, 0:2048], B["cwb"][:, ti * 4:ti * 4 + 1],
                                   None, ALU.mult).then_inc(cvv, 1)
            for kk in range(1, 4):
                p.vector.wait(cvv)
                p.vector.scalar_tensor_tensor(acc[:, :], xr[:, kk:kk + 2048],
                                              B["cwb"][:, ti * 4 + kk:ti * 4 + kk + 1],
                                              acc[:, :], ALU.mult, ALU.add).then_inc(cvv, 1)
            p.vector.tensor_copy(f3[:, ti * 3:ti * 3 + 3], xr[:, 3:6]).then_inc(cvv, 1)
            p.vector.tensor_copy(ag1s[:, ti * 3:ti * 3 + 3],
                                 xr[:, 2048:2051]).then_inc(cvv, 1)
            slot_free[ti] = cvv.n
            p.scalar.wait(cvv, slot_free[ti])
            if ti == 32:
                xo = xs2[:, r * TOK:(r + 1) * TOK]
                if ti >= 2:
                    p.scalar.wait(xst.sem(ti - 2))
                p.scalar.activation(xo, acc[:, :], AF.Silu,
                                    bias=B["cbb"][:, ti:ti + 1]).then_inc(evP, 1)
                ev_done[mi] = evP.n
                p.sync.wait(evP, ev_done[mi])
                p.sync.dma_start(B["btile"][64:128, :], xo[0:64, :]).then_inc(xst.sem(ti), 16)
                p.sync.dma_start(B["ctile"][64:128, :],
                                 xo[64:128, :]).then_inc(xst.sem(ti + 1), 16)
            else:
                xo = xs2[:, r * TOK:(r + 1) * TOK]
                if ti >= 2:
                    p.scalar.wait(xst.sem(ti - 2))
                p.scalar.activation(xo, acc[:, :], AF.Silu,
                                    bias=B["cbb"][:, ti:ti + 1]).then_inc(evP, 1)
                ev_done[mi] = evP.n
                p.sync.wait(evP, ev_done[mi])
                p.sync.dma_start(io["xhbm"][ti * 128:(ti + 1) * 128, :],
                                 xo).then_inc(xst.sem(ti), 16)
        else:
            for nb in range(NB):
                p.scalar.wait(mmP, mm_done[mi] - 3 + nb)
                p.scalar.activation(dtp[0:64, nb * 512:(nb + 1) * 512],
                                    ps[par * 4 + nb][0:64, :], AF.Copy).then_inc(evP, 1)
            ev_done[mi] = evP.n
    evP_all = evP.n
    cvv_all = cvv.n

    ag1 = p.sem(name + "_ag1")
    cc1 = p.sem(name + "_cc1")
    hl1 = p.sem(name + "_hl1")
    fxs = p.sem(name + "_fxs")
    p.gpsimd.wait(cvv, cvv_all)
    p.gpsimd.dma_start(io["ag1loc"][:, 0:99], ag1s[:, 0:99]).then_inc(ag1, 16)
    p.gpsimd.wait(ag1, 16)
    p.gpsimd.collective_compute(
        "AllGather", ALU.bypass, replica_groups=[list(range(NCORES))],
        ins=[io["ag1loc"].ap().opt()], outs=[io["ag1all"].ap().opt()]).then_inc(cc1)
    p.gpsimd.wait(cc1, 1)
    halo = B["halo"]
    hstart = vch.n
    p.vector.wait(cvv, cvv_all)
    p.vector.memset(halo[:, :], 0.0).then_inc(vch, 1)
    for s in range(NCORES):
        if s >= 1:
            p.gpsimd.wait(vch, hstart + 1 + s)
        p.gpsimd.dma_start(B["ag1g"][:, 0:99],
                           io["ag1all"][s * 128:(s + 1) * 128, 0:99]).then_inc(hl1, 16)
        p.vector.wait(hl1, 16 * (s + 1))
        p.vector.wait(vch, hstart + 1 + s)
        p.vector.scalar_tensor_tensor(halo[:, 0:99], B["ag1g"][:, 0:99],
                                      B["musel"][:, s:s + 1], halo[:, 0:99],
                                      ALU.mult, ALU.add).then_inc(vch, 1)
    fixv = p.sem(name + "_fixv")
    xst.wait_all(p.sync)
    for ti in range(33):
        xf = B["xfix"]
        p.vector.wait(vch)
        p.vector.tensor_copy(xf[:, 0:3], halo[:, ti * 3:ti * 3 + 3]).then_inc(vch, 1)
        p.vector.wait(vch)
        p.vector.tensor_copy(xf[:, 3:6], f3[:, ti * 3:ti * 3 + 3]).then_inc(vch, 1)
        p.vector.wait(vch)
        p.vector.wait(fixv, ti)
        p.vector.tensor_scalar(xf[:, 8:11], xf[:, 0:3], B["cwb"][:, ti * 4:ti * 4 + 1],
                               None, ALU.mult).then_inc(vch, 1)
        for kk in range(1, 4):
            p.vector.wait(vch)
            p.vector.scalar_tensor_tensor(xf[:, 8:11], xf[:, kk:kk + 3],
                                          B["cwb"][:, ti * 4 + kk:ti * 4 + kk + 1],
                                          xf[:, 8:11], ALU.mult, ALU.add).then_inc(vch, 1)
        p.scalar.wait(vch)
        if ti == 32:
            p.scalar.activation(B["xf3o"][:, 0:3], xf[:, 8:11], AF.Silu,
                                bias=B["cbb"][:, ti:ti + 1]).then_inc(fixv, 1)
            p.sync.wait(fixv, fixv.n)
            p.sync.dma_start(B["btile"][64:128, 0:3], B["xf3o"][0:64, 0:3]).then_inc(fxs, 16)
            p.sync.dma_start(B["ctile"][64:128, 0:3],
                             B["xf3o"][64:128, 0:3]).then_inc(fxs, 16)
        else:
            p.scalar.activation(B["xf3o"][:, 0:3], xf[:, 8:11], AF.Silu,
                                bias=B["cbb"][:, ti:ti + 1]).then_inc(fixv, 1)
            p.sync.wait(fixv, ti + 1)
            p.sync.dma_start(io["xhbm"][ti * 128:(ti + 1) * 128, 0:3],
                             B["xf3o"][:, 0:3]).then_inc(fxs, 16)
            p.scalar.wait(fxs)
    fixd = p.sem(name + "_fixd")
    p.scalar.nop().then_inc(fixd, 1)

    dtv = p.sem(name + "_dtv")
    p.scalar.wait(evP, evP_all)
    p.scalar.wait(cld, NCLD)
    p.scalar.wait(cn)
    d0, d2_ = B["dsc"][0:64, 0:2048], dtp[0:64, :]
    dmir = B["dsc"][64:128, 0:2048]
    p.scalar.activation(d0[:, :], d2_[:, :], AF.Abs, bias=B["dtb"][0:64, 0:1]).then_inc(sch, 1)
    p.scalar.wait(sch)
    p.scalar.activation(d0[:, :], d0[:, :], AF.Exp, scale=cp[0:64, 6:7]).then_inc(sch, 1)
    p.scalar.wait(sch)
    p.scalar.activation(d0[:, :], d0[:, :], AF.Ln, bias=cp[0:64, 11:12]).then_inc(sch, 1)
    p.scalar.wait(sch)
    p.scalar.activation(d2_[:, :], d2_[:, :], AF.Relu, bias=B["dtb"][0:64, 0:1]).then_inc(sch, 1)
    p.vector.wait(sch)
    p.vector.tensor_add(d2_[:, :], d0[:, :], d2_[:, :]).then_inc(dtv, 1)
    p.scalar.activation(B["nega"][:, 0:1], B["alog"][:, 0:1], AF.Exp).then_inc(sch, 1)
    p.vector.wait(sch)
    p.vector.tensor_scalar(B["nega"][:, 0:1], B["nega"][:, 0:1], cp[:, 6:7],
                           None, ALU.mult).then_inc(dtv, 1)
    dmv = p.sem(name + "_dmv")
    p.gpsimd.wait(dtv, 2)
    p.gpsimd.dma_start(dmir[:, :], d2_[:, :]).then_inc(dmv, 16)
    ac = dtp[64:128, :]
    p.vector.wait(dmv, 16)
    p.vector.tensor_scalar(ac[:, :], dmir[:, :], B["nega"][64:128, 0:1],
                           None, ALU.mult).then_inc(dtv, 1)
    p.vector.wait(dtv, 3)
    p.vector.memset(B["dsc"][64:128, 2048:2176] if False else dmir[:, 0:128],
                    0.0).then_inc(dtv, 1)
    zrow = dmir[:, 0:128]
    for c in range(MNC):
        p.vector.wait(dtv, 4 + c)
        p.vector.tensor_tensor_scan(ac[:, c * 128:(c + 1) * 128], ac[:, c * 128:(c + 1) * 128],
                                    zrow, cp[64:128, 7:8],
                                    ALU.add, ALU.add).then_inc(dtv, 1)
    p.scalar.wait(dtv, 4 + MNC)
    for c in range(MNC):
        p.scalar.activation(B["cdec"][0:64, 4096 + c:4097 + c],
                            ac[:, c * 128 + 127:c * 128 + 128], AF.Exp).then_inc(sch, 1)
    t25 = p.sem(name + "_t25")
    mt25 = p.sem(name + "_mt25")
    e25 = p.sem(name + "_e25")
    p.tensor.wait(evP, evP_all)
    for c in range(MNC):
        cs = slice(c * 128, (c + 1) * 128)
        p.scalar.wait(dtv, 4 + MNC)
        if c >= 1:
            p.scalar.wait(t25, 4 * c)
            p.tensor.wait(t25, 4 * c)
        p.scalar.activation(B["dsc"][0:64, 0:128], ac[:, cs], AF.Exp,
                            scale=cp[64:128, 6:7], bias=ac[:, c * 128 + 127:c * 128 + 128]
                            ).then_inc(sch, 1)
        p.scalar.wait(sch)
        p.scalar.activation(B["dsc"][0:64, 128:256], ac[:, cs], AF.Exp).then_inc(sch, 1)
        p.scalar.wait(sch)
        p.tensor.wait(sch)
        p.tensor.wait(mmc, 1)
        p.tensor.transpose(ps[6][:, 0:64], d2_[:, cs], B["idpair"][0:64, :]).then_inc(mt25, 1)
        p.tensor.transpose(ps[7][:, 0:64], B["dsc"][0:64, 0:128],
                           B["idpair"][0:64, :]).then_inc(mt25, 1)
        m2 = mt25.n
        p.scalar.wait(mt25, m2)
        p.scalar.activation(B["dtT"][:, c * 64:(c + 1) * 64], ps[6][:, 0:64],
                            AF.Copy).then_inc(t25, 1)
        p.scalar.activation(B["dsT"][:, c * 64:(c + 1) * 64], ps[7][:, 0:64],
                            AF.Copy).then_inc(t25, 1)
        p.tensor.wait(t25, 4 * c + 2)
        p.tensor.transpose(ps[6][:, 0:64], B["dsc"][0:64, 128:256],
                           B["idpair"][0:64, :]).then_inc(mt25, 1)
        p.tensor.transpose(ps[7][:, 0:64], ac[:, cs],
                           B["idpair"][64:128, :]).then_inc(mt25, 1)
        m4_ = mt25.n
        p.scalar.wait(mt25, m4_)
        p.scalar.activation(B["uT"][:, c * 64:(c + 1) * 64], ps[6][:, 0:64],
                            AF.Copy).then_inc(t25, 1)
        p.vector.wait(mt25, m4_)
        p.vector.tensor_scalar(B["nac"][:, c * 64:(c + 1) * 64],
                               ps[7][:, 0:64], cp[:, 6:7], cp[:, 12:13],
                               ALU.mult, ALU.add).then_inc(t25, 1)
    p.scalar.nop().then_inc(e25, 1)

    xcl = SemPool(p, name + "_xcl", 2)
    ssp = SemPool(p, name + "_ssp", 2)
    ev3 = p.sem(name + "_ev3")
    vv3 = p.sem(name + "_vv3")
    mm3 = p.sem(name + "_mm3")
    xcsl = B["xcsl"]
    spair = B["spair"]
    carry = B["carry"]
    mc = B["mc"]
    cdec = B["cdec"]
    p.tensor.wait(e25, 1)
    p.tensor.wait(msk, 3)
    p.tensor.wait(cldg, 32)
    gst.wait_all(p.scalar)
    xst.wait_all(p.vector)
    gst.wait_all(p.vector)
    xtev = {}
    sev = {}
    vxd = {}
    smm = {}
    vscan = {}
    mm_all_c = {}
    for c in range(MNC):
        r = c % 2
        if c >= 2:
            p.gpsimd.wait(mm3, mm_all_c[c - 2])
        if c == 0:
            p.gpsimd.wait(fxs)
            p.gpsimd.wait(fixd, 1)
            xst.wait_all(p.gpsimd)
        srcx = io["xhbm"][:, c * 128:(c + 1) * 128]
        srcx = srcx.rearrange("(k p) t -> p k t", p=128)
        p.gpsimd.dma_start(xcsl[:, r * 4096:(r + 1) * 4096].rearrange(
            "p (k t) -> p k t", k=32), srcx).then_inc(xcl.sem(c), 16)
        p.tensor.wait(xcl.sem(c))
        if c == 0:
            p.tensor.wait(evP, evP_all)
            p.tensor.wait(fixd, 1)
            p.tensor.wait(fxs)
            xst.wait_all(p.tensor)
        if c >= 2:
            p.tensor.wait(ev3, xtev[(c - 2, "bt")])
        p.tensor.transpose(ps[4 + r][:, :].bitcast(BF16)[:, 0:64],
                           B["btile"][64:128, c * 128:(c + 1) * 128],
                           B["idb"][64:128, 64:128]).then_inc(mm3, 1)
        btmm = mm3.n
        p.scalar.wait(mm3, btmm)
        if c >= 2:
            p.scalar.wait(vv3, vscan[c - 1])
        p.scalar.activation(B["bT"][:, r * 64:(r + 1) * 64],
                            ps[4 + r][:, :].bitcast(BF16)[:, 0:64],
                            AF.Copy).then_inc(ev3, 1)
        xtev[(c, "bt")] = ev3.n
        for h in range(MNH):
            hp, hr = divmod(h, 2)
            q = h % 2
            if h >= 2:
                p.tensor.wait(ev3, xtev[(c, h - 2)])
            elif c >= 1:
                p.tensor.wait(ev3, xtev[(c - 1, MNH - 2 + h)])
            p.tensor.transpose(ps[q][:, :].bitcast(BF16)[:, 0:64],
                               xcsl[64 * hr:64 * hr + 64,
                                    r * 4096 + hp * 128:r * 4096 + (hp + 1) * 128],
                               B["idb"][64 * hr:64 * hr + 64,
                                        64 * hr:64 * hr + 64]).then_inc(mm3, 1)
            mmtx = mm3.n
            p.scalar.wait(mm3, mmtx)
            if h >= 2:
                p.scalar.wait(vv3, vxd[(c, h - 2)])
            elif c >= 1:
                p.scalar.wait(vv3, vxd[(c - 1, MNH - 2 + h)])
            p.scalar.activation(B["xT"][:, q * 64:(q + 1) * 64],
                                ps[q][:, :].bitcast(BF16)[:, 0:64],
                                AF.Copy).then_inc(ev3, 1)
            xtev[(c, h)] = ev3.n
            p.vector.wait(ev3, xtev[(c, h)])
            p.vector.tensor_scalar(B["xdT"][:, q * 64:(q + 1) * 64],
                                   B["xT"][:, q * 64:(q + 1) * 64],
                                   B["dtT"][:, c * 64 + h:c * 64 + h + 1], None,
                                   ALU.mult).then_inc(vv3, 1)
            vxd[(c, h)] = vv3.n
            p.vector.wait(vv3)
            if h >= 2:
                p.vector.wait(mm3, smm[(c, h - 2)])
            elif c >= 1:
                p.vector.wait(mm3, smm[(c - 1, MNH - 2 + h)])
            p.vector.tensor_scalar(B["bdT"][:, q * 64:(q + 1) * 64],
                                   B["bT"][:, r * 64:(r + 1) * 64],
                                   B["dsT"][:, c * 64 + h:c * 64 + h + 1], None,
                                   ALU.mult).then_inc(vv3, 1)
            bdd = vv3.n
            p.tensor.wait(vv3, bdd)
            if h >= 2:
                p.tensor.wait(ev3, sev[(c, h - 2)])
            elif c >= 1:
                p.tensor.wait(ev3, sev[(c - 1, MNH - 2 + h)])
            inst = p.tensor.matmul(ps[2 + q][0:64, 0:64],
                                   B["bdT"][:, q * 64:(q + 1) * 64],
                                   B["xdT"][:, q * 64:(q + 1) * 64],
                                   start=True, stop=True)
            inst.then_inc(mm3, 1)
            smm[(c, h)] = mm3.n
            p.scalar.wait(mm3, smm[(c, h)])
            if c >= 1 and h == 0:
                p.scalar.wait(ssp.sem(c - 1))
                p.scalar.wait(vv3, vscan[c - 1])
            p.scalar.activation(spair[0:64, h * 64:(h + 1) * 64],
                                ps[2 + q][0:64, 0:64], AF.Copy).then_inc(ev3, 1)
            sev[(c, h)] = ev3.n
        p.sync.wait(ev3, sev[(c, MNH - 1)])
        p.sync.dma_start(io["shbm"][c * 64:(c + 1) * 64, :],
                         spair[0:64, :]).then_inc(ssp.sem(c), 16)
        p.tensor.wait(sch, None)
        for j in range(8):
            if not (c == 0 and j == 0):
                p.tensor.wait(ev3, None)
            inst = p.tensor.matmul(ps[6][0:64, :],
                                   cdec[0:64, 4096 + c:4097 + c].broadcast_to([64, 64]),
                                   B["rep64"][0:64, j * 512:(j + 1) * 512],
                                   start=True, stop=True)
            inst.then_inc(mm3, 1)
            p.scalar.wait(mm3, None)
            if c >= 1 and j == 0:
                p.scalar.wait(vv3, vscan[c - 1])
            p.scalar.activation(mc[0:64, j * 512:(j + 1) * 512], ps[6][0:64, :],
                                AF.Copy).then_inc(ev3, 1)
        mm_all_c[c] = mm3.n
        mcd = ev3.n
        p.vector.wait(ev3, mcd)
        if c == 0:
            p.vector.tensor_copy(carry[0:64, :], spair[0:64, :]).then_inc(vv3, 1)
        else:
            p.vector.tensor_mul(carry[0:64, :], carry[0:64, :],
                                mc[0:64, :]).then_inc(vv3, 1)
            p.vector.wait(vv3)
            p.vector.tensor_add(carry[0:64, :], carry[0:64, :],
                                spair[0:64, :]).then_inc(vv3, 1)
        vscan[c] = vv3.n
    mm3a_all = mm3.n
    vv3a_all = vv3.n

    ag2 = p.sem(name + "_ag2")
    cc2 = p.sem(name + "_cc2")
    sl2 = SemPool(p, name + "_sl2", 2)
    vin = p.sem(name + "_vin")
    p.gpsimd.wait(vv3, vv3a_all)
    p.gpsimd.dma_start(io["ag2loc"][:, :], carry[0:64, :]).then_inc(ag2, 16)
    p.gpsimd.wait(ag2, 16)
    p.gpsimd.collective_compute(
        "AllGather", ALU.bypass, replica_groups=[list(range(NCORES))],
        ins=[io["ag2loc"].ap().opt()], outs=[io["ag2all"].ap().opt()]).then_inc(cc2)
    p.gpsimd.wait(cc2, 1)
    slot = B["slotld"]
    p.vector.wait(vv3, vv3a_all)
    p.vector.wait(ag2, 16)
    p.vector.memset(carry[0:64, :], 0.0).then_inc(vin, 1)
    p.gpsimd.wait(mm3, mm3a_all)
    for s in range(NCORES):
        p.gpsimd.dma_start(slot[0:64, 0:4096],
                           io["ag2all"][s * 64:(s + 1) * 64, :]).then_inc(sl2.sem(s), 16)
        p.vector.wait(sl2.sem(s))
        p.vector.wait(vin, 1 + s)
        p.vector.scalar_tensor_tensor(carry[0:64, :], slot[0:64, 0:4096],
                                      B["musel"][0:64, s:s + 1], carry[0:64, :],
                                      ALU.mult, ALU.add).then_inc(vin, 1)
        p.gpsimd.wait(vin, 2 + s)
    vin_done = vin.n

    mm4 = p.sem(name + "_mm4")
    ev4 = p.sem(name + "_ev4")
    vv4 = p.sem(name + "_vv4")
    zmm = p.sem(name + "_zmm")
    ysp = SemPool(p, name + "_ysp", 2)
    sld = SemPool(p, name + "_sld", 2)
    cbf = SemPool(p, name + "_cbf", 2)
    ystg = B["ystg"]
    carrybf = B["carrybf"]
    gm = B["gm"]
    ted = {}
    ystt = {}
    lmd = {}
    pdn = {}
    ymm = {}
    cupd = {}
    for c in range(MNC):
        r = c % 2
        if c >= 1:
            p.gpsimd.wait(vv4, cupd[c - 1])
        if c < 2:
            p.gpsimd.wait(vv3, vv3a_all)
            ssp.wait_all(p.gpsimd)
            p.gpsimd.wait(mm3, mm3a_all)
            p.gpsimd.wait(vin, vin_done)
        if c >= 2:
            p.gpsimd.wait(vv4, cupd[c - 2])
        p.gpsimd.dma_start(spair[0:64, :],
                           io["shbm"][c * 64:(c + 1) * 64, :]).then_inc(sld.sem(c), 16)
        srcx = io["xhbm"][:, c * 128:(c + 1) * 128]
        srcx = srcx.rearrange("(k p) t -> p k t", p=128)
        p.gpsimd.dma_start(xcsl[:, r * 4096:(r + 1) * 4096].rearrange(
            "p (k t) -> p k t", k=32), srcx).then_inc(xcl.sem(c), 16)
        p.gpsimd.wait(vin, vin_done)
        if c >= 1:
            p.gpsimd.wait(zmm, 64 * c)
            p.gpsimd.wait(vv4, cupd[c - 1])
        p.gpsimd.dma_start(carrybf[:, :], carry[0:64, :]).then_inc(cbf.sem(c), 16)
        cbf_done = cbf.sem(c).n
        p.tensor.wait(mm3, mm3a_all)
        p.tensor.wait(sld.sem(c))
        p.tensor.wait(xcl.sem(c))
        if c >= 1:
            p.tensor.wait(vv4, ted[(c - 1, "gm")])
        inst = p.tensor.matmul(ps[6][:, 0:128], B["btile"][64:128, c * 128:(c + 1) * 128],
                               B["ctile"][64:128, c * 128:(c + 1) * 128],
                               start=True, stop=True)
        inst.then_inc(mm4, 1)
        p.vector.wait(mm4, None)
        p.vector.wait(msk, 3)
        p.vector.tensor_mul(gm[:, :], ps[6][:, 0:128], B["masksl"][:, :]).then_inc(vv4, 1)
        ted[(c, "gm")] = vv4.n
        gm_done = vv4.n
        p.tensor.wait(sch, None)
        for j in range(8):
            if not (c == 0 and j == 0):
                p.tensor.wait(ev4, None)
            inst = p.tensor.matmul(ps[7][0:64, :],
                                   cdec[0:64, 4096 + c:4097 + c].broadcast_to([64, 64]),
                                   B["rep64"][0:64, j * 512:(j + 1) * 512],
                                   start=True, stop=True)
            inst.then_inc(mm4, 1)
            p.scalar.wait(mm4, None)
            if c >= 1 and j == 0:
                p.scalar.wait(vv4, cupd[c - 1])
            p.scalar.activation(mc[0:64, j * 512:(j + 1) * 512], ps[7][0:64, :],
                                AF.Copy).then_inc(ev4, 1)
        mcd = ev4.n
        for h in range(MNH):
            hp, hr = divmod(h, 2)
            pb = (h % 2) * 3
            if c == 0 and h == 0:
                p.tensor.wait(t25, None)
            if h >= 2:
                p.tensor.wait(vv4, ted[(c, h - 2)])
            inst = p.tensor.matmul(
                ps[pb][:, 0:128],
                B["idpair"][64:128, h:h + 1].broadcast_to([64, 128]),
                dtp[64:128, c * 128:(c + 1) * 128],
                start=True, stop=True)
            inst.then_inc(mm4, 1)
            e_mm = mm4.n
            inst = p.tensor.transpose(
                ps[pb + 1][:, :].bitcast(BF16)[:, 0:64],
                xcsl[64 * hr:64 * hr + 64, r * 4096 + hp * 128:r * 4096 + (hp + 1) * 128],
                B["idb"][64 * hr:64 * hr + 64, 64 * hr:64 * hr + 64])
            inst.then_inc(mm4, 1)
            xt_mm = mm4.n
            q = h % 2
            p.vector.wait(mm4, e_mm)
            p.vector.wait(vv4, gm_done)
            pk = (c, h - 1) if h >= 1 else (c - 1, MNH - 1)
            if lmd.get(pk):
                p.vector.wait(ev4, lmd[pk])
            p.vector.scalar_tensor_tensor(B["tE"][:, :], ps[pb][:, 0:128],
                                          B["nac"][:, c * 64 + h:c * 64 + h + 1],
                                          B["masksl"][:, :], ALU.add,
                                          ALU.mult).then_inc(vv4, 1)
            ted[(c, h)] = vv4.n
            p.scalar.wait(vv4, ted[(c, h)])
            if pdn.get(pk):
                p.scalar.wait(vv4, pdn[pk])
            p.scalar.activation(B["lm"][:, :], B["tE"][:, :], AF.Exp,
                                bias=cp[:, 13:14]).then_inc(ev4, 1)
            lmd[(c, h)] = ev4.n
            p.vector.wait(ev4, lmd[(c, h)])
            if ymm.get(pk):
                p.vector.wait(mm4, ymm[pk])
            p.vector.tensor_mul(B["pmat"][:, :], B["lm"][:, :], gm[:, :]).then_inc(vv4, 1)
            p_done = vv4.n
            pdn[(c, h)] = p_done
            p.scalar.wait(mm4, xt_mm)
            if ystt.get((c, h - 2)):
                p.scalar.wait(vv4, ystt[(c, h - 2)])
            elif c >= 1 and h < 2:
                p.scalar.wait(vv4, ystt[(c - 1, MNH - 2 + h)])
            p.scalar.activation(B["xT"][:, q * 64:(q + 1) * 64],
                                ps[pb + 1][:, :].bitcast(BF16)[:, 0:64],
                                AF.Copy).then_inc(ev4, 1)
            xt_done = ev4.n
            p.vector.wait(ev4, xt_done)
            p.vector.tensor_scalar(B["xdT"][:, q * 64:(q + 1) * 64],
                                   B["xT"][:, q * 64:(q + 1) * 64],
                                   B["dtT"][:, c * 64 + h:c * 64 + h + 1], None,
                                   ALU.mult).then_inc(vv4, 1)
            xdt_done = vv4.n
            p.tensor.wait(vv4, max(p_done, xdt_done))
            inst = p.tensor.matmul(ps[pb + 1][:, 0:64], B["pmat"][:, :],
                                   B["xdT"][:, q * 64:(q + 1) * 64],
                                   start=True, stop=True)
            inst.then_inc(mm4, 1)
            y_mm = mm4.n
            ymm[(c, h)] = y_mm
            p.tensor.wait(cbf.sem(c), cbf_done)
            if h >= 2:
                p.tensor.wait(vv4, ystt[(c, h - 2)])
            inst = p.tensor.matmul(ps[pb + 2][:, 0:64],
                                   B["ctile"][64:128, c * 128:(c + 1) * 128],
                                   carrybf[:, h * 64:(h + 1) * 64],
                                   start=True, stop=True)
            inst.then_inc(zmm, 1)
            z_mm = zmm.n
            p.scalar.wait(mm4, y_mm)
            if h >= 2:
                p.scalar.wait(vv4, ystt[(c, h - 2)])
            p.scalar.activation(B["yd"][:, (h % 2) * 64:(h % 2) * 64 + 64],
                                ps[pb + 1][:, 0:64], AF.Copy).then_inc(ev4, 1)
            yd_done = ev4.n
            p.vector.wait(ev4, yd_done)
            p.vector.wait(zmm, z_mm)
            p.vector.scalar_tensor_tensor(B["yd"][:, (h % 2) * 64:(h % 2) * 64 + 64],
                                          ps[pb + 2][:, 0:64],
                                          B["uT"][:, c * 64 + h:c * 64 + h + 1],
                                          B["yd"][:, (h % 2) * 64:(h % 2) * 64 + 64],
                                          ALU.mult, ALU.add).then_inc(vv4, 1)
            p.vector.wait(vv4)
            p.vector.scalar_tensor_tensor(B["yt"][:, (h % 2) * 64:(h % 2) * 64 + 64],
                                          B["xT"][:, q * 64:(q + 1) * 64],
                                          B["dx"][:, h:h + 1],
                                          B["yd"][:, (h % 2) * 64:(h % 2) * 64 + 64],
                                          ALU.mult, ALU.add).then_inc(vv4, 1)
            ystt[(c, h)] = vv4.n
            p.tensor.wait(vv4, ystt[(c, h)])
            inst = p.tensor.transpose(ps[pb + 1][:, :].bitcast(BF16)[0:64, 0:128],
                                      B["yt"][:, (h % 2) * 64:(h % 2) * 64 + 64],
                                      B["idb"][:, :])
            inst.then_inc(mm4, 1)
            ybt = mm4.n
            p.scalar.wait(mm4, ybt)
            if hp < 16:
                dst = B["ma"][64 * hr:64 * hr + 64,
                              hp * 2048 + c * 128:hp * 2048 + (c + 1) * 128]
            else:
                if h == 32 and c >= 1:
                    p.scalar.wait(ysp.sem(c - 1))
                dst = ystg[64 * hr:64 * hr + 64, (hp - 16) * 128:(hp - 15) * 128]
            p.scalar.activation(dst, ps[pb + 1][:, :].bitcast(BF16)[0:64, 0:128],
                                AF.Copy).then_inc(ev4, 1)
        p.sync.wait(ev4, None)
        p.sync.dma_start(io["yhbm"][2048:4096, c * 128:(c + 1) * 128].rearrange(
            "(k p) t -> p k t", p=128),
            ystg[:, 0:2048].rearrange("p (k t) -> p k t", k=16)).then_inc(ysp.sem(c), 16)
        p.vector.wait(zmm, 64 * (c + 1))
        p.vector.wait(ev4, mcd)
        p.vector.tensor_mul(carry[0:64, :], carry[0:64, :], mc[0:64, :]).then_inc(vv4, 1)
        p.vector.wait(vv4)
        p.vector.tensor_add(carry[0:64, :], carry[0:64, :],
                            spair[0:64, :]).then_inc(vv4, 1)
        cupd[c] = vv4.n

    gld4 = SemPool(p, name + "_gld4", 2)
    yld4 = SemPool(p, name + "_yld4", 2)
    sq4 = p.sem(name + "_sq4")
    stm4 = p.sem(name + "_stm4")
    vg4 = p.sem(name + "_vg4")
    stp4 = p.sem(name + "_stp4")
    gsil = B["dtpair"].bitcast(BF16)
    ysq = B["xsq2"]
    ma, mb = B["ma"], B["mb"]
    yn_tile = lambda i: (ma[:, i * 2048:(i + 1) * 2048] if i < 16
                         else mb[:, (i - 16) * 2048:(i - 15) * 2048])
    ysp.wait_all(p.gpsimd)
    p.gpsimd.wait(vv4, cupd[MNC - 1])
    p.gpsimd.wait(zmm, None)
    p.sync.wait(mm4, None)
    gdone4 = {}
    for i in range(16, 32):
        p.gpsimd.dma_start(yn_tile(i),
                           io["yhbm"][2048 + (i - 16) * 128:2048 + (i - 15) * 128, :]
                           ).then_inc(yld4.sem(i), 16)
    yall = [s.n for s in yld4.sems]
    for i in range(32):
        rr = i % 2
        if i >= 2:
            p.sync.wait(vg4, gdone4[i - 2])
        p.sync.dma_start(gsil[:, rr * 2048:(rr + 1) * 2048],
                         io["gatehbm"][i * 128:(i + 1) * 128, :]).then_inc(gld4.sem(i), 16)
        p.scalar.wait(gld4.sem(i))
        p.scalar.activation(gsil[:, rr * 2048:(rr + 1) * 2048],
                            gsil[:, rr * 2048:(rr + 1) * 2048], AF.Silu).then_inc(sq4, 1)
        silu_done = sq4.n
        p.vector.wait(sq4, silu_done)
        if i >= 16:
            p.vector.wait(yld4.sems[0], yall[0])
            p.vector.wait(yld4.sems[1], yall[1])
        if i == 0:
            p.vector.wait(ev4, None)
        p.vector.tensor_mul(yn_tile(i), yn_tile(i),
                            gsil[:, rr * 2048:(rr + 1) * 2048]).then_inc(vg4, 1)
        p.vector.wait(vg4)
        if i >= 1:
            p.vector.wait(stm4, i)
        p.vector.tensor_mul(ysq[:, 0:2048], yn_tile(i), yn_tile(i)).then_inc(vg4, 1)
        gdone4[i] = vg4.n
        p.tensor.wait(vg4, gdone4[i])
        if i == 0:
            p.tensor.wait(ev4, None)
        for nb in range(NB):
            last = p.tensor.matmul(ps[4 + nb][:, :], B["ones128"][:, :],
                                   ysq[:, nb * 512:(nb + 1) * 512],
                                   start=(i == 0), stop=(i == 31))
        last.then_inc(stm4, 1)
    inv4 = B["invt"]
    for nb in range(NB):
        p.scalar.wait(stm4, 32)
        p.scalar.wait(cn)
        p.scalar.wait(stp4)
        p.scalar.activation(B["vart"][:, :], ps[4 + nb][:, :], AF.Copy,
                            scale=cp[:, 4:5]).then_inc(stp4, 1)
        p.scalar.wait(stp4)
        p.scalar.activation(B["vart"][:, :], B["vart"][:, :], AF.Ln,
                            bias=cp[:, 0:1]).then_inc(stp4, 1)
        p.scalar.wait(stp4)
        p.scalar.activation(inv4[:, nb * 512:(nb + 1) * 512], B["vart"][:, :], AF.Exp,
                            scale=cp[:, 1:2]).then_inc(stp4, 1)
    nrm4 = p.sem(name + "_nrm4")
    p.vector.wait(stp4, 12)
    for i in range(32):
        for nb in range(NB):
            sl = slice(nb * 512, (nb + 1) * 512)
            p.vector.tensor_mul(yn_tile(i)[:, sl], yn_tile(i)[:, sl],
                                inv4[:, sl]).then_inc(vg4, 1)
            p.vector.wait(vg4)
        lastv = p.vector.tensor_scalar(yn_tile(i), yn_tile(i), B["nw"][:, i:i + 1],
                                       None, ALU.mult)
    lastv.then_inc(nrm4, 1)

    wldO = [p.sem(name + "_wldO0"), p.sem(name + "_wldO1")]
    mmo = p.sem(name + "_mmo")
    evo = p.sem(name + "_evo")
    radd = p.sem(name + "_radd")
    rldp = SemPool(p, name + "_rld", 4)
    rstp = SemPool(p, name + "_rst", 4)
    opbs = [B["ipb"], B["dtpair"].bitcast(BF16)]
    p.tensor.wait(nrm4, 1)
    p.sync.wait(nrm4, 1)
    p.gpsimd.wait(nrm4, 1)
    for i in range(4):
        p.sync.dma_start(B["rbuf"][:, i * 512:(i + 1) * 512],
                         io["rpx"][0:128, i * 512:(i + 1) * 512]).then_inc(rldp.sem(i), 16)
    for mo in range(16):
        par = mo % 2
        if mo >= 2:
            p.gpsimd.wait(mmo, 4 * (mo - 1))
        for j in range(8):
            srcw = io["opT"][j * 512:(j + 1) * 512, mo * 128:(mo + 1) * 128]
            srcw = srcw.rearrange("(k p) c -> p k c", p=128)
            dst = opbs[par][:, j * 512:(j + 1) * 512]
            p.gpsimd.dma_start(dst.rearrange("p (k c) -> p k c", k=4),
                               srcw).then_inc(wldO[par], 16)
        p.tensor.wait(wldO[par])
        for nb in range(NB):
            i64 = mo * 4 + nb
            bp = i64 % 4
            if i64 >= 4:
                p.tensor.wait(evo, i64 - 3)
            for k in range(32):
                inst = p.tensor.matmul(
                    ps[bp][:, :],
                    opbs[par][:, k * 128:(k + 1) * 128],
                    yn_tile(k)[:, nb * 512:(nb + 1) * 512],
                    start=(k == 0), stop=(k == 31))
            inst.then_inc(mmo, 1)
            dsl = slice(bp * 512, (bp + 1) * 512)
            p.scalar.wait(mmo, i64 + 1)
            if i64 >= 4:
                p.scalar.wait(radd, i64 - 3)
            p.scalar.activation(B["dtmpm"][:, dsl], ps[bp][:, :], AF.Copy).then_inc(evo, 1)
            p.vector.wait(evo, i64 + 1)
            p.vector.wait(rldp.sem(i64))
            p.vector.tensor_add(B["rbuf"][:, dsl], B["dtmpm"][:, dsl],
                                B["rbuf"][:, dsl]).then_inc(radd, 1)
            p.sync.wait(radd, i64 + 1)
            p.sync.dma_start(io["rout"][mo * 128:(mo + 1) * 128,
                                        nb * 512:(nb + 1) * 512],
                             B["rbuf"][:, dsl]).then_inc(rstp.sem(i64), 16)
            nxt = i64 + 4
            if nxt < 64:
                nmo, nnb = divmod(nxt, 4)
                p.sync.wait(rstp.sem(i64))
                p.sync.dma_start(B["rbuf"][:, (nxt % 4) * 512:(nxt % 4 + 1) * 512],
                                 io["rpx"][nmo * 128:(nmo + 1) * 128,
                                           nnb * 512:(nnb + 1) * 512]
                                 ).then_inc(rldp.sem(nxt), 16)
    rstp.wait_all(p.gpsimd)
    gst.wait_all(p.gpsimd)
    return None




def alloc_mamba_bufs(p, B):
    ma = p.sbuf("ma", [128, 32768], BF16)
    mb = p.sbuf("mb", [128, 32768], BF16)
    B["ma"], B["mb"] = ma, mb
    B["xb"] = ma
    B["xbcr"] = mb[:, 0:8224].bitcast(F32)
    B["cacc"] = mb[:, 8224:12320].bitcast(F32)
    B["gt2"] = mb[:, 12320:16416]
    B["xs2"] = mb[:, 16416:20512]
    B["dsc"] = mb[:, 20512:24608].bitcast(F32)
    B["xcsl"] = mb[:, 0:8192]
    B["ystg"] = mb[:, 8192:10240]
    B["spair"] = mb[:, 10240:14336]
    B["carrybf"] = mb[64:128, 10240:14336]
    B["carry"] = mb[:, 14336:22528].bitcast(F32)
    B["mc"] = mb[:, 22528:30720].bitcast(F32)
    B["slotld"] = mb[:, 0:8192].bitcast(F32)
    btc = p.sbuf("btc", [128, 2048], BF16)
    B["btile"] = btc
    ctc = p.sbuf("ctc", [128, 2048], BF16)
    B["ctile"] = ctc
    B["dtpair"] = p.sbuf("dtpair", [128, 2048], F32)
    rp = p.sbuf("rp64", [128, 4160], F32)
    B["rep64"] = rp
    B["cdec"] = rp
    B["dtT"] = p.sbuf("dtT", [128, 1024], F32)
    B["dsT"] = p.sbuf("dsT", [128, 1024], F32)
    B["uT"] = p.sbuf("uT", [128, 1024], F32)
    B["nac"] = p.sbuf("nac", [128, 1024], F32)
    B["idb"] = p.sbuf("idbm", [128, 128], BF16)
    B["idpair"] = p.sbuf("idpair", [128, 64], F32)
    B["masksl"] = p.sbuf("masksl", [128, 128], BF16)
    B["onesf"] = p.sbuf("onesfm", [1, 128], F32)
    B["dx"] = p.sbuf("dxm", [128, 64], F32)
    B["nega"] = p.sbuf("negam", [128, 4], F32)
    B["dtb"] = p.sbuf("dtbm", [128, 1], F32)
    B["alog"] = p.sbuf("alogm", [128, 1], F32)
    B["d1"] = p.sbuf("d1m", [1, 64], F32)
    B["lnw"] = p.sbuf("lnwm", [128, 16], F32)
    B["nw"] = p.sbuf("nwm", [128, 32], F32)
    B["musel"] = p.sbuf("muselm", [128, 8], F32)
    B["cwb"] = p.sbuf("cwbm", [128, 132], F32)
    B["cbb"] = p.sbuf("cbbm", [128, 33], F32)
    B["gm"] = p.sbuf("gmm", [128, 128], BF16)
    B["tE"] = p.sbuf("tEm", [128, 128], F32)
    B["lm"] = p.sbuf("lmm", [128, 128], BF16)
    B["pmat"] = p.sbuf("pmatm", [128, 128], BF16)
    B["xT"] = p.sbuf("xTm", [128, 128], BF16)
    B["xdT"] = p.sbuf("xdTm", [128, 128], BF16)
    B["bdT"] = p.sbuf("bdTm", [128, 128], BF16)
    B["bT"] = p.sbuf("bTm", [128, 128], BF16)
    B["yd"] = p.sbuf("ydm", [128, 128], F32)
    B["yt"] = p.sbuf("ytm", [128, 128], BF16)
    B["ipb"] = p.sbuf("ipbm", [128, 4096], BF16)
    B["f3"] = p.sbuf("f3m", [128, 100], F32)
    B["ag1s"] = p.sbuf("ag1sm", [128, 100], F32)
    B["halo"] = mb[:, 0:200].bitcast(F32)
    B["ag1g"] = mb[:, 200:400].bitcast(F32)
    B["xfix"] = p.sbuf("xfixm", [128, 16], F32)
    B["xf3o"] = p.sbuf("xf3om", [128, 4], BF16)
    B["rbuf"] = B["mscratch"].bitcast(F32)
    B["dtmpm"] = B["xsq2"]
    return B


def build_mamba():
    p = Prog()
    io = {}
    io["rin"] = p.dram("rin", [C, TOK], F32, kind="ExternalInput")
    io["ipT"] = p.dram("ipT", [C, 8384], F32, kind="ExternalInput")
    io["opT"] = p.dram("opT", [4096, C], F32, kind="ExternalInput")
    io["cw2d"] = p.dram("cw2d", [128, 132], F32, kind="ExternalInput")
    io["cb2d"] = p.dram("cb2d", [128, 33], F32, kind="ExternalInput")
    io["dtb2d"] = p.dram("dtb2d", [64, 1], F32, kind="ExternalInput")
    io["alog2d"] = p.dram("alog2d", [64, 1], F32, kind="ExternalInput")
    io["d1r"] = p.dram("d1r", [1, 64], F32, kind="ExternalInput")
    io["lnw2d"] = p.dram("lnw2d", [128, 16], F32, kind="ExternalInput")
    io["nw2d"] = p.dram("nw2d", [128, 32], F32, kind="ExternalInput")
    io["id128"] = p.dram("id128", [128, 128], F32, kind="ExternalInput")
    io["id64"] = p.dram("id64", [64, 64], F32, kind="ExternalInput")
    io["rep64in"] = p.dram("rep64in", [64, 4096], F32, kind="ExternalInput")
    io["musel"] = p.dram("musel", [128, 8], F32, kind="ExternalInput")
    io["rout"] = p.dram("rout", [C, TOK], F32, kind="ExternalOutput")
    io["gatehbm"] = p.dram("gatehbm", [4096, TOK], BF16, kind="Internal")
    io["xhbm"] = p.dram("xhbm", [4096, TOK], BF16, kind="Internal")
    io["yhbm"] = p.dram("yhbm", [4096, TOK], BF16, kind="Internal")
    io["shbm"] = p.dram("shbm", [1024, 4096], BF16, kind="Internal")
    io["ag1loc"] = p.dram("ag1loc", [128, 99], F32, kind="Internal")
    io["ag1all"] = p.dram("ag1all", [1024, 99], F32, kind="Internal")
    io["ag2loc"] = p.dram("ag2loc", [64, 4096], F32, kind="Internal")
    io["ag2all"] = p.dram("ag2all", [512, 4096], F32, kind="Internal")
    io["rpx"] = p.dram("rpx", [C, TOK], F32, kind="Internal")
    B = alloc_common_bufs(p)
    alloc_mamba_bufs(p, B)
    sems = {"cn": emit_consts(p, B)}
    emit_mamba(p, B, sems, io, "m")
    return p.finish()


def mamba_host_inputs(ln_weight, in_proj_w, conv_w, conv_b, dt_bias, A_log, D,
                      norm_weight, out_proj_w):
    f32 = np.float32
    rep64 = np.zeros((64, 4096), f32)
    for h in range(64):
        rep64[h, h * 64:(h + 1) * 64] = 1.0
    return {
        "ipT": np.ascontiguousarray(in_proj_w.T, dtype=f32),
        "opT": np.ascontiguousarray(out_proj_w.T, dtype=f32),
        "cw2d": np.ascontiguousarray(
            conv_w.reshape(33, 128, 4).transpose(1, 0, 2).reshape(128, 132), dtype=f32),
        "cb2d": np.ascontiguousarray(conv_b.reshape(33, 128).T, dtype=f32),
        "dtb2d": np.ascontiguousarray(dt_bias.reshape(64, 1), dtype=f32),
        "alog2d": np.ascontiguousarray(A_log.reshape(64, 1), dtype=f32),
        "d1r": np.ascontiguousarray(D.reshape(1, 64), dtype=f32),
        "lnw2d": np.ascontiguousarray(ln_weight.reshape(16, 128).T, dtype=f32),
        "nw2d": np.ascontiguousarray(norm_weight.reshape(32, 128).T, dtype=f32),
        "id128": np.eye(128, dtype=f32),
        "id64": np.eye(64, dtype=f32),
        "rep64in": rep64,
    }


def mamba_core_inputs(core):
    musel = np.zeros((128, 8), np.float32)
    if core % 2 == 1:
        musel[:, core - 1] = 1.0
    return {"musel": musel}


def emit_rwkv(p, B, sems, io, name):
    nrm = emit_layernorm(p, B, sems, io["rin"], name + "ln")
    xb, ps, cp = B["xb"], B["ps"], B["cpool"]
    cn = sems["cn"]
    kvtf, kvth = B["kvtf"], B["kvth"]
    snum, sden = B["snum"], B["sden"]
    dbc, rwkv = B["dbc"], B["rwkv"]
    dec2, w0f, finl = B["dec2"], B["w0f"], B["finl"]
    rt = B["rt"]

    wld = [p.sem(name + "_wld0"), p.sem(name + "_wld1")]
    cld = p.sem(name + "_cld")
    cldg = p.sem(name + "_cldg")
    mmk = p.sem(name + "_mmk")
    evk = p.sem(name + "_evk")
    vp1 = p.sem(name + "_vp1")
    sp1 = p.sem(name + "_sp1")
    spst = SemPool(p, name + "_spst", 4)
    fst = p.sem(name + "_fst")
    ccs = p.sem(name + "_ccs")
    fld = p.sem(name + "_fld")
    vin = p.sem(name + "_vin")
    spld = SemPool(p, name + "_spld", 4)
    mmr = p.sem(name + "_mmr")
    evr = p.sem(name + "_evr")
    vp2 = p.sem(name + "_vp2")
    sp2 = p.sem(name + "_sp2")
    vch = p.sem(name + "_vch")
    sch = p.sem(name + "_sch")

    p.sync.dma_start(B["td2d"][:, :], io["td2d"][:, :]).then_inc(cld, 16)
    p.sync.dma_start(B["tf2d"][:, :], io["tf2d"][:, :]).then_inc(cld, 16)
    p.sync.dma_start(B["mu0"][:, :], io["mu0"][:, :]).then_inc(cld, 16)
    p.sync.dma_start(B["musel"][:, :], io["musel"][:, :]).then_inc(cld, 16)
    p.scalar.wait(cld, 64)
    p.scalar.wait(cn)
    p.scalar.activation(dec2[:, 0:16], B["td2d"][:, :], AF.Exp).then_inc(sch, 1)
    p.scalar.wait(sch)
    p.scalar.activation(dec2[:, 16:32], dec2[:, 0:16], AF.Exp,
                        scale=cp[:, 6:7]).then_inc(sch, 1)
    p.scalar.wait(sch)
    p.scalar.activation(w0f[:, :], B["tf2d"][:, :], AF.Exp,
                        scale=B["mu0"][:, 0:1]).then_inc(sp1, 1)

    for m in range(16):
        par = m % 2
        if m >= 2:
            p.gpsimd.wait(mmk, 8 * (m - 1))
        for wio, side in ((io["wkT"], 0), (io["wvT"], 1)):
            for j in range(4):
                srcw = wio[j * 512:(j + 1) * 512, m * 128:(m + 1) * 128]
                srcw = srcw.rearrange("(k p) c -> p k c", p=128)
                base = ((par * 2 + side) * 16 + j * 4) * 128
                dst = B["wkv1b"][:, base:base + 512]
                p.gpsimd.dma_start(dst.rearrange("p (k c) -> p k c", k=4),
                                   srcw).then_inc(wld[par], 16)
        p.tensor.wait(nrm, 1)
        p.tensor.wait(wld[par])
        if m >= 1:
            p.tensor.wait(evk, 8 * m)
        for side in range(2):
            for nb in range(NB):
                for k in range(KT):
                    inst = p.tensor.matmul(
                        ps[side * 4 + nb][:, :],
                        B["wkv1b"][:, ((par * 2 + side) * 16 + k) * 128:
                                   ((par * 2 + side) * 16 + k + 1) * 128],
                        xb[:, k * TOK + nb * 512:k * TOK + (nb + 1) * 512],
                        start=(k == 0), stop=(k == KT - 1))
                inst.then_inc(mmk, 1)
        for side in range(2):
            dstb = kvtf if side == 0 else kvth
            for nb in range(NB):
                p.scalar.wait(mmk, 8 * m + side * 4 + nb + 1)
                if m >= 1:
                    p.scalar.wait(vp1, m)
                    p.scalar.wait(spst.sem(2 * (m - 1)))
                    p.scalar.wait(spst.sem(2 * (m - 1) + 1))
                p.scalar.activation(
                    dstb[:, nb * 512:(nb + 1) * 512],
                    ps[side * 4 + nb][:, :], AF.Copy).then_inc(evk, 1)
        p.vector.wait(evk, 8 * (m + 1))
        p.vector.wait(cn)
        p.vector.tensor_scalar(kvtf[:, :], kvtf[:, :], cp[:, 9:10],
                               cp[:, 10:11], ALU.min, ALU.max).then_inc(vch, 1)
        p.scalar.wait(vch)
        p.scalar.activation(kvtf[:, :], kvtf[:, :], AF.Exp).then_inc(sp1, 1)
        p.vector.wait(sp1)
        p.vector.tensor_mul(kvtf[:, 0:1], kvtf[:, 0:1], w0f[:, m:m + 1]).then_inc(vch, 1)
        p.vector.wait(vch)
        p.vector.tensor_mul(kvth[:, :], kvtf[:, :], kvth[:, :]).then_inc(vch, 1)
        p.vector.wait(vch)
        p.vector.memset(dbc[:, :], 0.0).then_inc(vch, 1)
        p.vector.wait(vch)
        p.vector.tensor_scalar(dbc[:, :], dbc[:, :], dec2[:, 16 + m:17 + m],
                               None, ALU.add).then_inc(vch, 1)
        p.vector.wait(vch)
        p.vector.tensor_tensor_scan(snum[:, :], dbc[:, :], kvth[:, :],
                                    cp[:, 7:8], ALU.mult, ALU.add).then_inc(vch, 1)
        p.vector.wait(vch)
        p.vector.tensor_tensor_scan(sden[:, :], dbc[:, :], kvtf[:, :],
                                    cp[:, 7:8], ALU.mult, ALU.add).then_inc(vch, 1)
        p.vector.wait(vch)
        p.vector.tensor_copy(finl[:, 2 * m:2 * m + 1], snum[:, TOK - 1:TOK]).then_inc(vch, 1)
        p.vector.wait(vch)
        p.vector.tensor_copy(finl[:, 2 * m + 1:2 * m + 2],
                             sden[:, TOK - 1:TOK]).then_inc(vp1, 1)
        p.gpsimd.wait(vp1, m + 1)
        p.gpsimd.dma_start(io["wsp"][m * 128:(m + 1) * 128, :],
                           kvtf[:, :]).then_inc(spst.sem(2 * m), 16)
        p.gpsimd.dma_start(io["wvsp"][m * 128:(m + 1) * 128, :],
                           kvth[:, :]).then_inc(spst.sem(2 * m + 1), 16)
    p.gpsimd.wait(vp1, 16)
    p.gpsimd.dma_start(io["finloc"][:, :], finl[:, 0:32]).then_inc(fst, 16)
    p.gpsimd.wait(fst, 16)
    p.gpsimd.collective_compute(
        "AllGather", ALU.bypass, replica_groups=[list(range(NCORES))],
        ins=[io["finloc"].ap().opt()], outs=[io["finall"].ap().opt()]).then_inc(ccs)
    p.gpsimd.wait(ccs, 1)
    for s in range(NCORES):
        p.gpsimd.dma_start(finl[:, 32 + s * 32:32 + (s + 1) * 32],
                           io["finall"][s * 128:(s + 1) * 128, :]).then_inc(fld, 16)
    p.vector.wait(fld, 128)
    p.vector.memset(finl[:, 288:320], 0.0).then_inc(vch, 1)
    p.vector.wait(vch)
    for s in range(NCORES):
        p.vector.scalar_tensor_tensor(finl[:, 288:320], finl[:, 32 + s * 32:64 + s * 32],
                                      B["musel"][:, s:s + 1], finl[:, 288:320],
                                      ALU.mult, ALU.add).then_inc(vch, 1)
        p.vector.wait(vch)
    p.vector.nop().then_inc(vin, 1)

    spst.wait_all(p.gpsimd)
    p.gpsimd.wait(mmk, 128)
    p.tensor.wait(evk, 128)
    for m in range(16):
        par = m % 2
        if m >= 2:
            p.gpsimd.wait(mmr, 4 * (m - 1))
        for j in range(4):
            srcw = io["wrT"][j * 512:(j + 1) * 512, m * 128:(m + 1) * 128]
            srcw = srcw.rearrange("(k p) c -> p k c", p=128)
            base = (par * 16 + j * 4) * 128
            dst = B["wr2b"][:, base:base + 512]
            p.gpsimd.dma_start(dst.rearrange("p (k c) -> p k c", k=4),
                               srcw).then_inc(wld[par], 16)
        if m >= 1:
            p.gpsimd.wait(vp2, m)
        p.gpsimd.dma_start(kvtf[:, :], io["wsp"][m * 128:(m + 1) * 128, :]
                           ).then_inc(spld.sem(2 * m), 16)
        p.gpsimd.dma_start(kvth[:, :], io["wvsp"][m * 128:(m + 1) * 128, :]
                           ).then_inc(spld.sem(2 * m + 1), 16)
        p.tensor.wait(wld[par])
        if m >= 1:
            p.tensor.wait(evr, 4 * m)
        for nb in range(NB):
            inst = None
            for k in range(KT):
                inst = p.tensor.matmul(
                    ps[4 + nb][:, :],
                    B["wr2b"][:, (par * 16 + k) * 128:(par * 16 + k + 1) * 128],
                    xb[:, k * TOK + nb * 512:k * TOK + (nb + 1) * 512],
                    start=(k == 0), stop=(k == KT - 1))
            inst.then_inc(mmr, 1)
        for nb in range(NB):
            p.scalar.wait(mmr, 4 * m + nb + 1)
            if m >= 1:
                p.scalar.wait(vp2, m)
            p.scalar.activation(rt[:, nb * 512:(nb + 1) * 512], ps[4 + nb][:, :],
                                AF.Sigmoid).then_inc(evr, 1)
        p.vector.wait(vin, 1)
        p.vector.wait(spld.sem(2 * m))
        p.vector.wait(spld.sem(2 * m + 1))
        p.vector.memset(dbc[:, :], 0.0).then_inc(vch, 1)
        p.vector.wait(vch)
        p.vector.tensor_scalar(dbc[:, :], dbc[:, :], dec2[:, 16 + m:17 + m],
                               None, ALU.add).then_inc(vch, 1)
        p.vector.wait(vch)
        p.vector.tensor_tensor_scan(snum[:, :], dbc[:, :], kvth[:, :],
                                    finl[:, 288 + 2 * m:289 + 2 * m],
                                    ALU.mult, ALU.add).then_inc(vch, 1)
        p.vector.wait(vch)
        p.vector.tensor_tensor_scan(sden[:, :], dbc[:, :], kvtf[:, :],
                                    finl[:, 289 + 2 * m:290 + 2 * m],
                                    ALU.mult, ALU.add).then_inc(vch, 1)
        p.vector.wait(vch)
        p.vector.tensor_scalar(sden[:, :], sden[:, :], cp[:, 8:9], None,
                               ALU.add).then_inc(vch, 1)
        p.vector.wait(vch)
        p.vector.reciprocal(sden[:, :], sden[:, :]).then_inc(vch, 1)
        p.vector.wait(vch)
        p.vector.tensor_mul(snum[:, :], snum[:, :], sden[:, :]).then_inc(vch, 1)
        p.vector.wait(vch)
        p.vector.wait(evr, 4 * (m + 1))
        p.vector.tensor_mul(rwkv[:, m * TOK:(m + 1) * TOK], rt[:, :],
                            snum[:, :]).then_inc(vp2, 1)

    rbuf, dtmp = B["rbuf"], B["dtmp"]
    mwo = p.sem(name + "_mwo")
    evo = p.sem(name + "_evo")
    radd = p.sem(name + "_radd")
    rldp = SemPool(p, name + "_rld", 4)
    rstp = SemPool(p, name + "_rst", 4)
    p.tensor.wait(vp2, 16)
    p.gpsimd.wait(mmr, 64)
    p.sync.wait(vp2, 16)
    for i in range(4):
        p.sync.dma_start(rbuf[:, i * 512:(i + 1) * 512],
                         io["rin"][(i // 4) * 128:(i // 4) * 128 + 128,
                                   (i % 4) * 512:(i % 4 + 1) * 512]).then_inc(rldp.sem(i), 16)
    for mo in range(16):
        par = mo % 2
        if mo >= 2:
            p.gpsimd.wait(mwo, 4 * (mo - 1))
        for j in range(4):
            srcw = io["woT"][j * 512:(j + 1) * 512, mo * 128:(mo + 1) * 128]
            srcw = srcw.rearrange("(k p) c -> p k c", p=128)
            base = (par * 16 + j * 4) * 128
            dst = B["wr2b"][:, base:base + 512]
            p.gpsimd.dma_start(dst.rearrange("p (k c) -> p k c", k=4),
                               srcw).then_inc(wld[par], 16)
        p.tensor.wait(wld[par])
        for nb in range(NB):
            i64 = mo * 4 + nb
            bp = i64 % 4
            if i64 >= 4:
                p.tensor.wait(evo, i64 - 3)
            for k in range(KT):
                inst = p.tensor.matmul(
                    ps[bp][:, :],
                    B["wr2b"][:, (par * 16 + k) * 128:(par * 16 + k + 1) * 128],
                    rwkv[:, k * TOK + nb * 512:k * TOK + (nb + 1) * 512],
                    start=(k == 0), stop=(k == KT - 1))
            inst.then_inc(mwo, 1)
            dsl = slice(bp * 512, (bp + 1) * 512)
            p.scalar.wait(mwo, i64 + 1)
            if i64 >= 4:
                p.scalar.wait(radd, i64 - 3)
            p.scalar.activation(dtmp[:, dsl], ps[bp][:, :], AF.Copy).then_inc(evo, 1)
            p.vector.wait(evo, i64 + 1)
            p.vector.wait(rldp.sem(i64))
            p.vector.tensor_add(rbuf[:, dsl], dtmp[:, dsl],
                                rbuf[:, dsl]).then_inc(radd, 1)
            p.sync.wait(radd, i64 + 1)
            p.sync.dma_start(io["rout"][mo * 128:(mo + 1) * 128,
                                        nb * 512:(nb + 1) * 512],
                             rbuf[:, dsl]).then_inc(rstp.sem(i64), 16)
            nxt = i64 + 4
            if nxt < 64:
                nmo, nnb = divmod(nxt, 4)
                p.sync.wait(rstp.sem(i64))
                p.sync.dma_start(rbuf[:, (nxt % 4) * 512:(nxt % 4 + 1) * 512],
                                 io["rin"][nmo * 128:(nmo + 1) * 128,
                                           nnb * 512:(nnb + 1) * 512]
                                 ).then_inc(rldp.sem(nxt), 16)
    rstp.wait_all(p.gpsimd)
    return None


def alloc_rwkv_bufs(p, B):
    B["xb"] = p.sbuf("xb", [128, KT * TOK], BF16)
    B["rwkv"] = p.sbuf("rwkv", [128, KT * TOK], BF16)
    B["kvtf"] = p.sbuf("kvtf", [128, TOK], F32)
    B["kvth"] = p.sbuf("kvth", [128, TOK], BF16)
    B["snum"] = p.sbuf("snum", [128, TOK], F32)
    B["sden"] = p.sbuf("sden", [128, TOK], F32)
    dbuf = p.sbuf("dbuf", [128, TOK], F32)
    B["dbc"] = dbuf
    B["rbuf"] = dbuf
    B["rt"] = p.sbuf("rt", [128, TOK], BF16)
    B["wkv1b"] = p.sbuf("wkv1b", [128, 2 * 2 * 16 * 128], BF16)
    B["wr2b"] = B["wkv1b"][:, 0:2 * 16 * 128]
    B["td2d"] = p.sbuf("td2ds", [128, 16], F32)
    B["tf2d"] = p.sbuf("tf2ds", [128, 16], F32)
    B["mu0"] = p.sbuf("mu0s", [128, 1], F32)
    B["musel"] = p.sbuf("musels", [128, 8], F32)
    B["dec2"] = p.sbuf("dec2", [128, 32], F32)
    B["w0f"] = p.sbuf("w0f", [128, 16], F32)
    B["finl"] = p.sbuf("finl", [128, 320], F32)
    B["dtmp"] = p.sbuf("dtmpr", [128, 4 * 512], BF16)
    return B


def build_rwkv():
    p = Prog()
    io = {}
    io["rin"] = p.dram("rin", [C, TOK], F32, kind="ExternalInput")
    io["wkT"] = p.dram("wkT", [C, C], F32, kind="ExternalInput")
    io["wvT"] = p.dram("wvT", [C, C], F32, kind="ExternalInput")
    io["wrT"] = p.dram("wrT", [C, C], F32, kind="ExternalInput")
    io["woT"] = p.dram("woT", [C, C], F32, kind="ExternalInput")
    io["td2d"] = p.dram("td2d", [128, 16], F32, kind="ExternalInput")
    io["tf2d"] = p.dram("tf2d", [128, 16], F32, kind="ExternalInput")
    io["mu0"] = p.dram("mu0", [128, 1], F32, kind="ExternalInput")
    io["musel"] = p.dram("musel", [128, 8], F32, kind="ExternalInput")
    io["rout"] = p.dram("rout", [C, TOK], F32, kind="ExternalOutput")
    io["wsp"] = p.dram("wsp", [C, TOK], BF16, kind="Internal")
    io["wvsp"] = p.dram("wvsp", [C, TOK], BF16, kind="Internal")
    io["finloc"] = p.dram("finloc", [128, 32], F32, kind="Internal")
    io["finall"] = p.dram("finall", [128 * NCORES, 32], F32, kind="Internal")
    B = alloc_common_bufs(p)
    alloc_rwkv_bufs(p, B)
    sems = {"cn": emit_consts(p, B)}
    emit_rwkv(p, B, sems, io, "r")
    return p.finish()


def rwkv_host_inputs(Wk, Wv, Wr, Wo, time_decay, time_first):
    f32 = np.float32
    out = {
        "wkT": np.ascontiguousarray(Wk.T, dtype=f32),
        "wvT": np.ascontiguousarray(Wv.T, dtype=f32),
        "wrT": np.ascontiguousarray(Wr.T, dtype=f32),
        "woT": np.ascontiguousarray(Wo.T, dtype=f32),
        "td2d": np.ascontiguousarray(time_decay.reshape(16, 128).T, dtype=f32),
        "tf2d": np.ascontiguousarray(time_first.reshape(16, 128).T, dtype=f32),
    }
    return out


def rwkv_core_inputs(core):
    mu0 = np.full((128, 1), 1.0 if core % 2 == 0 else 0.0, np.float32)
    musel = np.zeros((128, 8), np.float32)
    if core % 2 == 1:
        musel[:, core - 1] = 1.0
    return {"mu0": mu0, "musel": musel}


def l0_host_inputs(gate_w, gate_bias, shared_gw, shared_dw, up_w, down_w):
    f32 = np.float32
    w1 = np.concatenate([shared_gw.T] + [up_w[e].T for e in range(8)], axis=1)
    w2 = np.concatenate([shared_dw.T] + [down_w[e].T for e in range(8)], axis=0)
    gw2d = np.ascontiguousarray(
        gate_w.T.reshape(16, 128, 8).transpose(1, 0, 2).reshape(128, 128))
    sel8 = np.zeros((8, 1024), f32)
    for e in range(8):
        sel8[e, e * 128:(e + 1) * 128] = 1.0
    return {
        "w1": np.ascontiguousarray(w1, dtype=f32),
        "w2": np.ascontiguousarray(w2, dtype=f32),
        "gw2d": gw2d.astype(f32, copy=False),
        "gbias2d": np.ascontiguousarray(gate_bias.reshape(8, 1), dtype=f32),
        "id128": np.eye(128, dtype=f32),
        "sel8i": sel8,
    }


def l1_host_inputs(w_gate, w_up, w_down, s_gate, s_up, s_down):
    return {
        "wg": w_gate, "wu": w_up, "wd": w_down,
        "sg2d": scale_cols(s_gate), "su2d": scale_cols(s_up),
        "sd2d": scale_cols(s_down),
    }


_PROGS = {}


def _get_prog(name):
    if name not in _PROGS:
        _PROGS[name] = {"l0ffn": build_l0ffn, "mamba": build_mamba,
                        "l1ffn": build_l1ffn, "rwkv": build_rwkv}[name]()
    return _PROGS[name]


def kernel(**inputs):
    g = inputs
    shards = shard_tokens(np.ascontiguousarray(g["L0_attn_hidden_states"],
                                               dtype=np.float32))
    hi_l0 = l0_host_inputs(g["L0_ffn_gate_w"], g["L0_ffn_gate_bias"],
                           g["L0_ffn_shared_gw"], g["L0_ffn_shared_dw"],
                           g["L0_ffn_up_w"], g["L0_ffn_down_w"])
    hi_mb = mamba_host_inputs(g["L0_attn_ln_weight"], g["L0_attn_in_proj_w"],
                              g["L0_attn_conv_w"], g["L0_attn_conv_b"],
                              g["L0_attn_dt_bias"], g["L0_attn_A_log"],
                              g["L0_attn_D"], g["L0_attn_norm_weight"],
                              g["L0_attn_out_proj_w"])
    hi_l1 = l1_host_inputs(g["L1_ffn_w_gate"], g["L1_ffn_w_up"], g["L1_ffn_w_down"],
                           g["L1_ffn_s_gate"], g["L1_ffn_s_up"], g["L1_ffn_s_down"])
    hi_rw = rwkv_host_inputs(g["L1_attn_Wk"], g["L1_attn_Wv"], g["L1_attn_Wr"],
                             g["L1_attn_Wo"], g["L1_attn_time_decay"],
                             g["L1_attn_time_first"])
    r = shards
    cores = list(range(NCORES))

    def launch(progname, hi, extra=None):
        nonlocal r
        nc = _get_prog(progname)
        in_maps = []
        for c in cores:
            m = dict(hi, rin=r[c])
            if extra is not None:
                m.update(extra(c))
            in_maps.append(m)
        res = run_bass_kernel_spmd(nc, in_maps, core_ids=cores)
        r = [np.asarray(res.results[c]["rout"]) for c in cores]

    launch("l0ffn", hi_l0)
    launch("mamba", hi_mb, mamba_core_inputs)
    launch("l0ffn", hi_l0)
    launch("l1ffn", hi_l1)
    launch("rwkv", hi_rw, rwkv_core_inputs)
    launch("l1ffn", hi_l1)
    return unshard_tokens(r)
